# version 20
# baseline (speedup 1.0000x reference)
import numpy as np
import concourse.bass as bass
import concourse.mybir as mybir
from concourse.bass_utils import run_bass_kernel_spmd

F32 = mybir.dt.float32
BF16 = mybir.dt.bfloat16
AF = mybir.ActivationFunctionType
OP = mybir.AluOpType
AX = mybir.AxisListType

B, SEQ, D = 2, 8192, 1024
T = 2048
W = 256
XW = W + 1 + T + 1
TW = W + T
KT = D // 128
DI = 2048
CT = DI // 128
NE = 10
IH = 1408
KI = IH // 128
GU = 2816
PJ = 11
MO = 8
EPS = 1e-5

CH_X = [(0, 512), (512, 512), (1024, 512), (1536, 512), (2048, 258)]
CH_G = [(0, 512), (512, 512), (1024, 512), (1536, 512), (2048, 256)]
CH_T = [(0, 512), (512, 512), (1024, 512), (1536, 512)]

A_XSC = 0
A_OMG = 36864
A_CT1 = 39168
A_CT2 = 41472
A_XSP = 43776
A_ZS = 46082
A_WA = 50178
A_WO = 52226
A_X4 = 56324
AO_EVEN = 46082
AO_ODD = 36864
X4B = 40960
X2A = 43776
X2B = 50178
F_MOE = 0
F_H = 16384
F_DB = 38912
F_WG = 50176
F_WU = 52224
F_WB = 54272
F_XM = 56320
H_RB = 60424
H_MRB = 60936
H_T2 = 61448
ABF_N = 61960
AF_M = 1280
AF_S = 1792
AF_T = 2304
AF32_N = 2816
RG = 0
RPM = 2304
RMEAN = 4608
RMS = 5120
ROWS_N = 5632


def _build(stop=None, dbg=False):
    nc = bass.Bass("TRN2", target_bir_lowering=False)
    dr = {}
    for nm, sh in [
        ("xw", [D, XW]), ("pmask", [1, TW]),
        ("w_in_T", [D, 2 * DI]), ("conv_w", [128, 48]), ("conv_b", [128, 16]),
        ("d_param", [128, 16]), ("dt_b", [128, 16]), ("xpw", [64, DI]),
        ("dtw", [DI, 64]), ("w_out_T", [DI, D]), ("gw_T", [D, 8]),
        ("seg_T", [D, 1]), ("gu_all", [NE, D, GU]), ("down_all", [NE, IH, D]),
        ("ident", [128, 128]), ("ones_row", [65, 128]), ("ones_col", [128, 1]),
    ]:
        dr[nm] = nc.dram_tensor(nm, sh, F32, kind="ExternalInput").ap()
    out_d = nc.dram_tensor("out", [D, T], F32, kind="ExternalOutput").ap()
    xmid_d = nc.dram_tensor("xmid_hbm", [D, T], BF16,
                            kind="ExternalOutput" if dbg else "Internal").ap()
    dbg_d = {}
    if dbg:
        for nm, sh, dt in [("dbg_abf", [128, ABF_N], BF16),
                           ("dbg_xn", [128, KT * XW], BF16),
                           ("dbg_rows", [128, ROWS_N], F32),
                           ("dbg_afp", [128, AF32_N], F32),
                           ("dbg_sm", [128, 24], F32),
                           ("dbg_tok", [128, 128], F32),
                           ("dbg_scs", [128, 128], F32),
                           ("dbg_wsc", [128, 128], F32),
                           ("dbg_wet", [8, T], F32)]:
            dbg_d[nm] = nc.dram_tensor(nm, sh, dt, kind="ExternalOutput").ap()

    import contextlib
    with contextlib.ExitStack() as ctx:
        sb = lambda n, s, t: ctx.enter_context(nc.sbuf_tensor(n, s, t))
        ps = lambda n, s: ctx.enter_context(nc.psum_tensor(n, s, F32))

        ABF = sb("ABF", [128, ABF_N], BF16)
        AFP = sb("AFP", [128, AF32_N], F32)
        XN = sb("XN", [128, KT * XW], BF16)
        ROWS = sb("ROWS", [128, ROWS_N], F32)
        ident_f = sb("ident_f", [128, 128], F32)
        onesr_f = sb("onesr_f", [65, 128], F32)
        onesc_bf = sb("onesc_bf", [128, 1], BF16)
        conv_f = sb("conv_f", [128, 48], F32)
        cb_f = sb("cb_f", [128, 16], F32)
        cb_bf = sb("cb_bf", [128, 16], BF16)
        dpar_f = sb("dpar_f", [128, 16], F32)
        dtb_bf = sb("dtb_bf", [128, 16], BF16)
        gw_bf = sb("gw_bf", [128, KT * 8], BF16)
        seg_bf = sb("seg_bf", [128, KT], BF16)
        sm_f = sb("sm_f", [128, 24], F32)
        sm_bf = sb("sm_bf", [128, 18], BF16)
        tok_sc = sb("tok_sc", [128, 128], F32)
        sc_scr = sb("sc_scr", [128, 128], F32)
        w_sc = sb("w_sc", [128, 128], F32)
        st8 = sb("st8", [128, 32], F32)
        lgch = sb("lgch", [8, 512], F32)
        w_eT = sb("w_eT", [8, T], F32)

        PSA = [ps("PSA%d" % i, [128, 512]) for i in range(8)]

        sem_names = ["PE", "AC", "DV", "GPC", "GX", "WA0", "WA1", "WO0", "WO1",
                     "GU0", "GU1", "GU2", "GDN", "SPC", "SPL", "WRO", "OS0", "OS1"]
        sems = {s: ctx.enter_context(nc.semaphore(s)) for s in sem_names}

        class Ctr:
            __slots__ = ("h", "n")
            def __init__(self, h): self.h = h; self.n = 0
        C = {s: Ctr(sems[s]) for s in sem_names}
        PE, AC, DV = C["PE"], C["AC"], C["DV"]
        GPC, GX, GDN = C["GPC"], C["GX"], C["GDN"]
        WA = [C["WA0"], C["WA1"]]; WO = [C["WO0"], C["WO1"]]
        GUS = [C["GU0"], C["GU1"], C["GU2"]]
        SPC, SPL, WRO = C["SPC"], C["SPL"], C["WRO"]
        OS = [C["OS0"], C["OS1"]]

        ops = {"sp": [], "gp": [], "pe": [], "ac": [], "dv": []}

        def EW(q, ctr, v):
            if v > 0:
                ops[q].append(lambda e, h=ctr.h, vv=v: e.wait_ge(h, vv))

        def EI(q, fn, ctr=None, k=1):
            if ctr is None:
                ops[q].append(fn)
                return None
            ctr.n += k
            ops[q].append(lambda e, f=fn, h=ctr.h, kk=k: f(e).then_inc(h, kk))
            return ctr.n

        def DMA(q, ctr, out_ap, in_ap):
            return EI(q, lambda e, o=out_ap, i=in_ap: e.dma_start(out=o, in_=i), ctr, 16)

        def MM(psum_ap, lhsT_ap, rhs_ap, start, stop, ctr=None):
            return EI("pe", lambda e, a=psum_ap, b=lhsT_ap, c=rhs_ap, s=start, t=stop:
                      e.matmul(a, b, c, start=s, stop=t), ctr)

        def ACT(out_ap, in_ap, func, bias=0.0, scale=1.0, ctr=None):
            return EI("ac", lambda e, o=out_ap, i=in_ap, f=func, bi=bias, sc=scale:
                      e.activation(o, i, f, bias=bi, scale=sc), ctr)

        def TT(q, out_ap, a_ap, b_ap, op, ctr=None):
            return EI(q, lambda e, o=out_ap, a=a_ap, b=b_ap, p=op:
                      e.tensor_tensor(o, a, b, p), ctr)

        def TS(q, out_ap, in_ap, s1, s2, op0, op1, ctr=None):
            return EI(q, lambda e, o=out_ap, i=in_ap, a=s1, b=s2, p0=op0, p1=op1:
                      e.tensor_scalar(o, i, a, b, p0, p1), ctr)

        def STT(q, out_ap, in0_ap, scal, in1_ap, op0, op1, ctr=None):
            return EI(q, lambda e, o=out_ap, i=in0_ap, s=scal, j=in1_ap, p0=op0, p1=op1:
                      e.scalar_tensor_tensor(o, i, s, j, p0, p1), ctr)

        def CPY(q, out_ap, in_ap, ctr=None):
            return EI(q, lambda e, o=out_ap, i=in_ap: e.tensor_copy(o, i), ctr)

        def abf(off, n):  return ABF[:, off:off + n]
        def afp(off, n):  return AFP[:, off:off + n]
        def xnv(kt, c0, n):  return XN[:, kt * XW + c0: kt * XW + c0 + n]
        def xsc(ct, c0, n):  return abf(A_XSC + ct * TW + c0, n)
        def xwv(kt, c0, n):  return abf(kt * XW + c0, n)

        def ones_r(base):  return onesr_f[base:base + 1, :]

        xpw_bf = ABF[0:64, 54274:54274 + DI]
        dtw_bf = ABF[:, A_WO:A_WO + 1024]

        r_g    = lambda c0, n: ROWS[0:1, RG + c0: RG + c0 + n]
        r_pmk  = lambda c0, n: ROWS[0:1, RPM + c0: RPM + c0 + n]
        r_mean = lambda n: ROWS[0:1, RMEAN: RMEAN + n]
        r_ms   = lambda n: ROWS[0:1, RMS: RMS + n]

        bank_ev = [None] * 8
        def bank_wait(i):
            st = bank_ev[i]
            if st is not None:
                EW("pe", st[0], st[1])
                bank_ev[i] = None
        def bank_set(i, ctr, val):
            bank_ev[i] = (ctr, val)

        x2_ev = [0, 0]

        def ln_stats_chunk(ap_fn, c0, cn, x2off):
            x2slots = [x2off, x2off + 512]
            bank_wait(6); bank_wait(7)
            for mo in range(MO):
                sl = x2slots[mo % 2]
                if x2_ev[mo % 2]:
                    EW("ac", PE, x2_ev[mo % 2])
                vac = ACT(abf(sl, cn), ap_fn(mo, c0, cn), AF.Square, ctr=AC)
                MM(PSA[6][0:1, 0:cn], onesc_bf[:, :], ap_fn(mo, c0, cn),
                   mo == 0, mo == MO - 1)
                EW("pe", AC, vac)
                vpe = MM(PSA[7][0:1, 0:cn], onesc_bf[:, :], abf(sl, cn), mo == 0,
                         mo == MO - 1, ctr=PE)
                x2_ev[mo % 2] = vpe
            EW("ac", PE, PE.n)
            ACT(r_mean(cn), PSA[6][0:1, 0:cn], AF.Copy, scale=1.0 / D)
            vms = ACT(r_ms(cn), PSA[7][0:1, 0:cn], AF.Copy, scale=1.0 / D, ctr=AC)
            bank_set(6, AC, vms); bank_set(7, AC, vms)
            return vms

        ln_war = [0]

        def ln_finish_chunk(cn, vms):
            EW("pe", AC, vms)
            if ln_war[0]:
                EW("pe", DV, ln_war[0])
            bank_wait(4); bank_wait(5)
            MM(PSA[4][:, 0:cn], ones_r(0), r_mean(cn), True, True)
            vb = MM(PSA[5][:, 0:cn], ones_r(0), r_ms(cn), True, True, ctr=PE)
            EW("ac", PE, vb)
            ACT(afp(AF_M, cn), PSA[4][:, 0:cn], AF.Copy)
            vev = ACT(afp(AF_S, cn), PSA[5][:, 0:cn], AF.Copy, ctr=AC)
            bank_set(4, AC, vev); bank_set(5, AC, vev)
            EW("dv", AC, vev)
            TT("dv", afp(AF_T, cn), afp(AF_M, cn), afp(AF_M, cn), OP.mult)
            TT("dv", afp(AF_S, cn), afp(AF_S, cn), afp(AF_T, cn), OP.subtract)
            vvar = EI("dv", lambda e, o=afp(AF_S, cn), i=afp(AF_S, cn):
                      e.tensor_scalar_add(o, i, EPS), DV)
            EW("ac", DV, vvar)
            vstd = ACT(afp(AF_S, cn), afp(AF_S, cn), AF.Sqrt, ctr=AC)
            EW("dv", AC, vstd)
            EI("dv", lambda e, o=afp(AF_S, cn), i=afp(AF_S, cn): e.reciprocal(o, i))
            TT("dv", abf(H_MRB, cn), afp(AF_M, cn), afp(AF_S, cn), OP.mult, ctr=DV)
            vfin = CPY("dv", abf(H_RB, cn), afp(AF_S, cn), ctr=DV)
            EW("dv", DV, vfin)
            return vfin

        def finish():
            if dbg:
                for c in [PE, AC, DV]:
                    EW("sp", c, c.n)
                for dst, src in [
                    (dbg_d["dbg_abf"], ABF[:, :]), (dbg_d["dbg_xn"], XN[:, :]),
                    (dbg_d["dbg_rows"], ROWS[:, :]), (dbg_d["dbg_afp"], AFP[:, :]),
                    (dbg_d["dbg_sm"], sm_f[:, :]), (dbg_d["dbg_tok"], tok_sc[:, :]),
                    (dbg_d["dbg_scs"], sc_scr[:, :]), (dbg_d["dbg_wsc"], w_sc[:, :]),
                    (dbg_d["dbg_wet"], w_eT[:, :]),
                ]:
                    DMA("sp", SPL, dst, src)
            EW("sp", SPL, SPL.n)
            blk = ctx.enter_context(nc.Block())
            for q, attr in [("sp", "sync"), ("gp", "gpsimd"), ("pe", "tensor"),
                            ("ac", "scalar"), ("dv", "vector")]:
                def section(eng, _q=q):
                    for f in ops[_q]:
                        f(eng)
                getattr(blk, attr)(section)
            return nc

        for (o, i) in [(ident_f[:, :], dr["ident"]), (onesr_f[:, :], dr["ones_row"]),
                       (conv_f[:, :], dr["conv_w"]), (cb_f[:, :], dr["conv_b"]),
                       (dpar_f[:, :], dr["d_param"]), (r_pmk(0, TW), dr["pmask"])]:
            DMA("sp", SPC, o, i)
        v_sp_consts = SPC.n

        DMA("gp", GPC, onesc_bf[:, :], dr["ones_col"])
        DMA("gp", GPC, cb_bf[:, :], dr["conv_b"])
        DMA("gp", GPC, dtb_bf[:, :], dr["dt_b"])
        DMA("gp", GPC, gw_bf.ap().rearrange("p (k e) -> p k e", k=KT),
            dr["gw_T"].rearrange("(k p) e -> p k e", p=128))
        EI("gp", lambda e, o=seg_bf.ap().rearrange("p (k o) -> p k o", k=KT),
           i=dr["seg_T"].rearrange("(k p) o -> p k o", p=128):
           e.dma_start(out=o, in_=i, allow_slow_non_contiguous=True), GPC, 16)
        v_gp_small = GPC.n

        wa_ld = [0] * 32
        wa_rel = [0] * 32
        def emit_wa_load(k):
            ct = k % 16
            col0 = (0 if k < 16 else DI) + ct * 128
            if k >= 2:
                EW("gp", PE, wa_rel[k - 2])
            slot = A_WA + (k % 2) * 1024
            wa_ld[k] = DMA("gp", WA[k % 2],
                           abf(slot, 1024).rearrange("p (k m) -> p k m", k=KT),
                           dr["w_in_T"][:, col0:col0 + 128]
                           .rearrange("(k p) m -> p k m", p=128))

        v_xw = [0] * len(CH_X)
        def emit_xw(ci):
            c0, cn = CH_X[ci]
            v_xw[ci] = DMA("gp", GX,
                           ABF.ap()[:, 0:KT * XW]
                           .rearrange("p (k t) -> p k t", k=KT)[:, :, c0:c0 + cn],
                           dr["xw"][:, c0:c0 + cn]
                           .rearrange("(k p) t -> p k t", p=128))
        emit_xw(0); emit_xw(1); emit_xw(2)
        emit_wa_load(0)
        emit_xw(3); emit_xw(4)
        emit_wa_load(1)
        DMA("gp", GPC, xpw_bf[:, :], dr["xpw"])
        DMA("gp", GPC, dtw_bf.rearrange("p (t r) -> p t r", t=16),
            dr["dtw"].rearrange("(t p) r -> p t r", p=128))
        v_gp_consts = GPC.n

        EW("pe", GPC, v_gp_small)
        EW("pe", SPC, v_sp_consts)
        v_ln1_done_dv = 0
        v_ln1_ci = [0] * len(CH_X)
        for ci, (c0, cn) in enumerate(CH_X):
            EW("ac", GX, v_xw[ci])
            EW("pe", GX, v_xw[ci])
            vms = ln_stats_chunk(lambda mo, a, b: xwv(mo, a, b), c0, cn, X2A)
            ln_finish_chunk(cn, vms)
            for kt in range(KT):
                TT("dv", abf(H_T2, cn), xwv(kt, c0, cn), abf(H_RB, cn), OP.mult)
                v_ln1_done_dv = TT("dv", xnv(kt, c0, cn), abf(H_T2, cn),
                                   abf(H_MRB, cn), OP.subtract, ctr=DV)
            ln_war[0] = v_ln1_done_dv
            v_ln1_ci[ci] = v_ln1_done_dv

        if stop == "p0":
            return finish()
        conv_done = [0] * CT
        EW("dv", SPC, v_sp_consts)
        for ct in range(CT):
            slot = A_WA + (ct % 2) * 1024
            EW("pe", WA[ct % 2], wa_ld[ct])
            for ci in range(len(CH_X)):
                bank_wait(ci)
            vstop = [0] * len(CH_X)
            for kt in range(KT):
                for ci, (c0, cn) in enumerate(CH_X):
                    if ct == 0 and kt == 0:
                        EW("pe", DV, v_ln1_ci[ci])
                    v = MM(PSA[ci][:, 0:cn], abf(slot + kt * 128, 128),
                           xnv(kt, c0, cn), kt == 0, kt == KT - 1,
                           ctr=PE if kt == KT - 1 else None)
                    if kt == KT - 1:
                        vstop[ci] = v
            wa_rel[ct] = vstop[-1]
            ev_last = 0
            for ci, (c0, cn) in enumerate(CH_X):
                EW("ac", PE, vstop[ci])
                if ci == 0 and ct > 0:
                    EW("ac", DV, conv_done[ct - 1])
                ev_last = ACT(abf(A_XSP + c0, cn), PSA[ci][:, 0:cn], AF.Copy, ctr=AC)
                bank_set(ci, AC, ev_last)
            if ct + 2 < 32:
                emit_wa_load(ct + 2)
            EW("dv", AC, ev_last)
            cw = lambda k: conv_f[:, ct * 3 + k: ct * 3 + k + 1]
            TS("dv", abf(A_CT1, TW), abf(A_XSP, TW), cw(0), cb_f[:, ct:ct + 1],
               OP.mult, OP.add)
            STT("dv", abf(A_CT2, TW), abf(A_XSP + 1, TW), cw(1), abf(A_CT1, TW),
                OP.mult, OP.add)
            conv_done[ct] = STT("dv", xsc(ct, 0, TW), abf(A_XSP + 2, TW), cw(2),
                                abf(A_CT2, TW), OP.mult, OP.add, ctr=DV)

        EW("pe", GPC, v_gp_consts)
        bank_wait(3)
        for t in range(16):
            MM(PSA[3][0:64, 0:1], dtw_bf[:, t * 64:(t + 1) * 64], onesc_bf[:, :],
               t == 0, t == 15, ctr=PE if t == 15 else None)
        EW("ac", PE, PE.n)
        vwm_ev = ACT(sm_f[0:64, 0:1], PSA[3][0:64, 0:1], AF.Copy, scale=1.0 / DI, ctr=AC)
        EW("dv", AC, vwm_ev)
        vwmbf = CPY("dv", sm_bf[0:64, 16:17], sm_f[0:64, 0:1], ctr=DV)
        EW("pe", DV, vwmbf)
        for ct in range(CT):
            MM(PSA[3][:, 4 + ct:5 + ct], xpw_bf[0:64, ct * 128:(ct + 1) * 128],
               sm_bf[0:64, 16:17], True, True, ctr=PE if ct == CT - 1 else None)
        EW("ac", PE, PE.n)
        vv_ev = ACT(sm_f[:, 4:20], PSA[3][:, 4:20], AF.Copy, ctr=AC)
        bank_set(3, AC, vv_ev)
        EW("dv", AC, vv_ev)
        vvbf = CPY("dv", sm_bf[:, 0:16], sm_f[:, 4:20], ctr=DV)
        EW("pe", DV, vvbf)
        bank_wait(6)
        for t in range(16):
            MM(PSA[6][0:1, 0:1], dtb_bf[:, t:t + 1], onesc_bf[:, :], t == 0, t == 15)
        for t in range(16):
            MM(PSA[6][0:1, 1:2], cb_bf[:, t:t + 1], sm_bf[:, t:t + 1], t == 0,
               t == 15, ctr=PE if t == 15 else None)
        EW("ac", PE, PE.n)
        ACT(sm_f[0:1, 20:21], PSA[6][0:1, 0:1], AF.Copy, scale=1.0 / DI)
        vbm1 = ACT(sm_f[0:1, 21:22], PSA[6][0:1, 1:2], AF.Copy, ctr=AC)
        bank_set(6, AC, vbm1)
        EW("dv", AC, vbm1)
        v_bm_dv = TT("dv", sm_f[0:1, 20:21], sm_f[0:1, 20:21], sm_f[0:1, 21:22],
                     OP.add, ctr=DV)

        if stop == "p0b":
            return finish()
        EW("pe", DV, conv_done[CT - 1])
        g_ev = 0
        EW("ac", DV, v_bm_dv)
        bank_wait(6)
        for (c0, cn) in CH_G:
            if g_ev:
                EW("pe", AC, g_ev)
            for ct in range(CT):
                MM(PSA[6][0:1, 0:cn], sm_bf[:, ct:ct + 1], xsc(ct, c0, cn),
                   ct == 0, ct == CT - 1, ctr=PE if ct == CT - 1 else None)
            EW("ac", PE, PE.n)
            g_ev = ACT(r_g(c0, cn), PSA[6][0:1, 0:cn], AF.Sigmoid,
                       bias=sm_f[0:1, 20:21], ctr=AC)
        bank_set(6, AC, g_ev)

        if stop == "p1":
            return finish()
        EW("dv", AC, g_ev)
        EW("dv", SPC, v_sp_consts)
        TT("dv", AFP[0:1, 0:TW], r_g(0, TW), r_pmk(0, TW), OP.mult)
        vomg = TT("dv", r_pmk(0, TW), r_pmk(0, TW), AFP[0:1, 0:TW],
                  OP.subtract, ctr=DV)
        EW("pe", DV, vomg)
        EW("pe", DV, ln_war[0])
        bc_ev_dv = 0
        bc_ev_ac = 0
        for (c0, cn) in CH_G:
            bank_wait(4); bank_wait(5)
            MM(PSA[4][:, 0:cn], ones_r(0), r_g(c0, cn), True, True)
            vb = MM(PSA[5][:, 0:cn], ones_r(0), r_pmk(c0, cn), True, True, ctr=PE)
            EW("dv", PE, vb)
            bc_ev_dv = CPY("dv", afp(c0, cn), PSA[4][:, 0:cn], ctr=DV)
            bank_set(4, DV, bc_ev_dv)
            EW("ac", PE, vb)
            bc_ev_ac = ACT(abf(A_OMG + c0, cn), PSA[5][:, 0:cn], AF.Copy, ctr=AC)
            bank_set(5, AC, bc_ev_ac)

        if stop == "p2":
            return finish()
        zs_ready = [0] * CT
        gate_done = [0] * CT
        for ct in range(CT):
            k = 16 + ct
            slot = A_WA + (k % 2) * 1024
            zslot = A_ZS + (ct % 2) * 2048
            EW("pe", WA[k % 2], wa_ld[k])
            for ci in range(4):
                bank_wait(ci)
            vstop = [0] * 4
            for kt in range(KT):
                for ci, (c0, cn) in enumerate(CH_T):
                    v = MM(PSA[ci][:, 0:cn], abf(slot + kt * 128, 128),
                           xnv(kt, 257 + c0, cn), kt == 0, kt == KT - 1,
                           ctr=PE if kt == KT - 1 else None)
                    if kt == KT - 1:
                        vstop[ci] = v
            wa_rel[k] = vstop[-1]
            for ci, (c0, cn) in enumerate(CH_T):
                EW("ac", PE, vstop[ci])
                if ct >= 2 and ci == 0:
                    EW("ac", DV, gate_done[ct - 2])
                vz = ACT(abf(zslot + c0, cn), PSA[ci][:, 0:cn], AF.Silu, ctr=AC)
                bank_set(ci, AC, vz)
            zs_ready[ct] = AC.n
            if k + 2 < 32:
                emit_wa_load(k + 2)
            if ct == 0:
                EW("dv", AC, bc_ev_ac)
                EW("dv", DV, bc_ev_dv)
            TT("dv", abf(A_CT1, TW), xsc(ct, 0, TW), abf(A_OMG, TW), OP.mult)
            EI("dv", lambda e, o=abf(A_CT2, TW), g=afp(0, TW), b=abf(A_CT1, TW):
               e.tensor_tensor_scan(o, g, b, 0.0, OP.mult, OP.add))
            EW("dv", AC, zs_ready[ct])
            TT("dv", abf(A_CT2 + W, T), abf(A_CT2 + W, T), abf(zslot, T), OP.mult)
            gate_done[ct] = STT("dv", xsc(ct, W, T), xsc(ct, W, T),
                                dpar_f[:, ct:ct + 1], abf(A_CT2 + W, T),
                                OP.mult, OP.add, ctr=DV)
        v_gate_all = gate_done[CT - 1]

        if stop == "p3":
            return finish()
        wo_ld = {}
        wo_rel = {}
        def emit_wo_load(idx):
            mo = idx % MO
            if idx >= 2:
                EW("gp", PE, wo_rel[idx - 2])
            slot = A_WO + (idx % 2) * 2048
            wo_ld[idx] = DMA("gp", WO[idx % 2],
                             abf(slot, 2048).rearrange("p (k m) -> p k m", k=CT),
                             dr["w_out_T"][:, mo * 128:(mo + 1) * 128]
                             .rearrange("(k p) m -> p k m", p=128))
        emit_wo_load(0); emit_wo_load(1)

        ao_off = [AO_EVEN, AO_ODD]
        x4_off = [A_X4, X4B]
        spill_vs = [0, 0, 0, 0]
        vres_chunk = [0, 0, 0, 0]
        ao_stop = [[0] * MO for _ in range(4)]
        ao_ev = [0, 0, 0, 0]
        vx4s = [0, 0, 0, 0]

        def emit_outproj_chunk(no, stages=None, sno=-1):
            c0, cn = CH_T[no]
            if no >= 2:
                EW("gp", SPL, spill_vs[no - 2])
                EW("gp", DV, xn2_dones[no - 2])
            vx4s[no] = DMA("gp", GX,
                           abf(x4_off[no % 2], 4096).rearrange("p (k t) -> p k t", k=KT),
                           dr["xw"][:, 257 + c0: 257 + c0 + cn]
                           .rearrange("(k p) t -> p k t", p=128))
            for mo in range(MO):
                if stages is not None and mo % 2 == 0 and mo // 2 < len(stages):
                    pass
                idx = no * MO + mo
                EW("pe", WO[idx % 2], wo_ld[idx])
                if idx == 0:
                    EW("pe", DV, v_gate_all)
                slot = A_WO + (idx % 2) * 2048
                bank_wait(mo)
                vpe = 0
                for kt in range(CT):
                    vpe = MM(PSA[mo][:, 0:cn], abf(slot + kt * 128, 128),
                             xsc(kt, W + c0, cn), kt == 0, kt == CT - 1,
                             ctr=PE if kt == CT - 1 else None)
                wo_rel[idx] = vpe
                ao_stop[no][mo] = vpe
                if idx + 2 < 32:
                    emit_wo_load(idx + 2)
                EW("ac", PE, vpe)
                if mo == 0 and no >= 2:
                    EW("ac", DV, vres_chunk[no - 2])
                ao_ev[no] = ACT(abf(ao_off[no % 2] + mo * 512, cn), PSA[mo][:, 0:cn],
                                AF.Copy, ctr=AC)
                bank_set(mo, AC, ao_ev[no])
                if stages is not None and mo % 2 == 1 and mo // 2 < len(stages):
                    stages[mo // 2](sno)

        xn2_dones = [0, 0, 0, 0]
        lnst = {}

        def ln_stageA(no):
            c0, cn = CH_T[no]
            off = ao_off[no % 2]
            EW("pe", AC, ao_ev[no])
            lnst["vms"] = ln_stats_chunk(
                lambda mo, a, b: abf(off + mo * 512 + (a - c0), b), c0, cn, X2B)

        def ln_stageB(no):
            c0, cn = CH_T[no]
            off = ao_off[no % 2]
            ln_finish_chunk(cn, lnst["vms"])
            EW("dv", GX, vx4s[no])
            vres = 0
            for mo in range(MO):
                TT("dv", abf(H_T2, cn), abf(off + mo * 512, cn), abf(H_RB, cn), OP.mult)
                TT("dv", abf(H_T2, cn), abf(H_T2, cn), abf(H_MRB, cn), OP.subtract)
                vres = TT("dv", abf(x4_off[no % 2] + mo * 512, cn),
                          abf(x4_off[no % 2] + mo * 512, cn), abf(H_T2, cn),
                          OP.add, ctr=DV)
            ln_war[0] = vres
            vres_chunk[no] = vres
            EW("sp", DV, vres)
            spill_vs[no] = DMA("sp", SPL,
                               xmid_d[:, c0:c0 + cn].rearrange("(k p) t -> p k t", p=128),
                               abf(x4_off[no % 2], 4096).rearrange("p (k t) -> p k t", k=KT))

        def ln_stageC(no):
            c0, cn = CH_T[no]
            vres = vres_chunk[no]
            EW("ac", DV, vres)
            EW("pe", DV, vres)
            lnst["vms"] = ln_stats_chunk(
                lambda mo, a, b: abf(x4_off[no % 2] + mo * 512 + (a - c0), b),
                c0, cn, X2B)

        def ln_stageD(no):
            c0, cn = CH_T[no]
            ln_finish_chunk(cn, lnst["vms"])
            xn2_done = 0
            for kt in range(KT):
                TT("dv", abf(H_T2, cn), abf(x4_off[no % 2] + kt * 512, cn),
                   abf(H_RB, cn), OP.mult)
                xn2_done = TT("dv", xnv(kt, c0, cn), abf(H_T2, cn), abf(H_MRB, cn),
                              OP.subtract, ctr=DV)
            ln_war[0] = xn2_done
            xn2_dones[no] = xn2_done

        lgst = {"lg_ev": 0, "v_tp": 0}

        def emit_logits(no):
            c0, cn = CH_T[no]
            EW("pe", DV, xn2_dones[no])
            bank_wait(3)
            bank_wait(5)
            for kt in range(KT):
                MM(PSA[5][0:8, 0:cn], gw_bf[:, kt * 8:(kt + 1) * 8], xnv(kt, c0, cn),
                   kt == 0, kt == KT - 1, ctr=PE if kt == KT - 1 else None)
            EW("dv", PE, PE.n)
            lgst["lg_ev"] = CPY("dv", lgch[:, 0:cn], PSA[5][0:8, 0:cn], ctr=DV)
            bank_set(5, DV, lgst["lg_ev"])
            EW("pe", DV, lgst["lg_ev"])
            vtp = 0
            for i in range(4):
                tt = no * 4 + i
                vtp = EI("pe", lambda e, o=PSA[3][:, i * 8:(i + 1) * 8],
                         s=lgch[0:8, i * 128:(i + 1) * 128],
                         idn=ident_f[0:8, 0:8]:
                         e.transpose(o, s, idn), ctr=PE)
            EW("dv", PE, vtp)
            lgst["v_tp"] = CPY("dv", tok_sc[:, no * 32:(no + 1) * 32],
                               PSA[3][:, 0:32], ctr=DV)
            bank_set(3, DV, lgst["v_tp"])

        stages = [ln_stageA, ln_stageB, ln_stageC, ln_stageD]
        for no in range(4):
            emit_outproj_chunk(no, stages if no >= 1 else None, no - 1)
            if no >= 2:
                emit_logits(no - 2)
        for st_fn in stages:
            st_fn(3)
        emit_logits(2)
        emit_logits(3)
        v_spill_all = spill_vs[3]
        v_xn2_dv = xn2_dones[3]

        if stop in ("p4", "p5"):
            EW("pe", DV, v_xn2_dv)
            return finish()
        xn2 = xnv
        v_tok = lgst["v_tp"]
        def step_bar(v):
            EW("dv", DV, v)
        EW("ac", DV, v_tok)
        vexp = ACT(sc_scr[:, :], tok_sc[:, :], AF.Exp, ctr=AC)
        EW("dv", AC, vexp)
        v = 0
        for tt in range(16):
            v = EI("dv", lambda e, o=st8[:, tt:tt + 1],
                   i=sc_scr[:, tt * 8:(tt + 1) * 8]: e.reduce_sum(o, i, axis=AX.X), DV)
        step_bar(v)
        v = EI("dv", lambda e, o=st8[:, 16:32], i=st8[:, 0:16]: e.reciprocal(o, i), DV)
        step_bar(v)
        for tt in range(16):
            v = EI("dv", lambda e, o=sc_scr[:, tt * 8:(tt + 1) * 8],
                   i=sc_scr[:, tt * 8:(tt + 1) * 8], s=st8[:, 16 + tt:17 + tt]:
                   e.tensor_scalar_mul(o, i, s), DV)
        step_bar(v)
        for tt in range(16):
            v = EI("dv", lambda e, o=st8[:, tt:tt + 1],
                   i=sc_scr[:, tt * 8:(tt + 1) * 8]: e.reduce_max(o, i, axis=AX.X), DV)
        step_bar(v)
        for tt in range(16):
            v = EI("dv", lambda e, o=w_sc[:, tt * 8:(tt + 1) * 8],
                   i=sc_scr[:, tt * 8:(tt + 1) * 8], s=st8[:, tt:tt + 1]:
                   e.tensor_single_scalar(o, i, s, OP.is_ge), DV)
        step_bar(v)
        v = STT("dv", w_sc[:, :], w_sc[:, :], -1e30, sc_scr[:, :], OP.mult, OP.add,
                ctr=DV)
        step_bar(v)
        for tt in range(16):
            v = EI("dv", lambda e, o=st8[:, tt:tt + 1],
                   i=w_sc[:, tt * 8:(tt + 1) * 8]: e.reduce_max(o, i, axis=AX.X), DV)
        step_bar(v)
        for tt in range(16):
            v = EI("dv", lambda e, o=w_sc[:, tt * 8:(tt + 1) * 8],
                   i=sc_scr[:, tt * 8:(tt + 1) * 8], s=st8[:, tt:tt + 1]:
                   e.tensor_single_scalar(o, i, s, OP.is_ge), DV)
        step_bar(v)
        vsc_dv = TT("dv", w_sc[:, :], w_sc[:, :], sc_scr[:, :], OP.mult, ctr=DV)
        wT_ev = 0
        EW("pe", DV, vsc_dv)
        bank_wait(3)
        for no in range(4):
            if wT_ev:
                EW("pe", DV, wT_ev)
            vtp = 0
            for i in range(4):
                tt = no * 4 + i
                vtp = EI("pe", lambda e, o=PSA[3][0:8, i * 128:(i + 1) * 128],
                         s=w_sc[:, tt * 8:(tt + 1) * 8], idn=ident_f[:, :]:
                         e.transpose(o, s, idn), ctr=PE)
            EW("dv", PE, vtp)
            wT_ev = CPY("dv", w_eT[:, no * 512:(no + 1) * 512], PSA[3][0:8, 0:512],
                        ctr=DV)
        bank_set(3, DV, wT_ev)
        sw_ev = 0
        bank_wait(6)
        for (c0, cn) in CH_T:
            if sw_ev:
                EW("pe", AC, sw_ev)
            for kt in range(KT):
                MM(PSA[6][0:1, 0:cn], seg_bf[:, kt:kt + 1], xn2(kt, c0, cn),
                   kt == 0, kt == KT - 1, ctr=PE if kt == KT - 1 else None)
            EW("ac", PE, PE.n)
            sw_ev = ACT(r_pmk(c0, cn), PSA[6][0:1, 0:cn], AF.Sigmoid, ctr=AC)
        bank_set(6, AC, sw_ev)

        if stop == "p5b":
            return finish()
        v_attn_pe = PE.n
        GUSLOT = [50176, 52224]
        gu_ld = {}
        gu_rel = {}
        def emit_gu_load(e, j):
            idx = e * PJ + j
            if idx >= 2:
                EW("gp", PE, gu_rel[idx - 2])
            else:
                EW("gp", PE, v_attn_pe)
            slot = GUSLOT[idx % 2]
            gu_ld[idx] = DMA("gp", GUS[idx % 2],
                             abf(slot, 2048).rearrange("p (k m) -> p k m", k=KT),
                             dr["gu_all"][e, :, j * 256:(j + 1) * 256]
                             .rearrange("(k p) m -> p k m", p=128))

        db_ld = [0] * NE
        db_rel = [0] * NE
        def emit_db_tile(e, kt):
            if kt == 0:
                if e == 0:
                    EW("gp", PE, v_attn_pe)
                else:
                    EW("gp", PE, db_rel[e - 1])
            db_ld[e] = DMA("gp", GDN, abf(F_DB + kt * 1024, 1024),
                           dr["down_all"][e, kt * 128:(kt + 1) * 128, :])

        emit_gu_load(0, 0); emit_gu_load(0, 1)
        moe_add_dv = 0
        hs_war_dv = [0, 0, 0, 0]
        eo_war = [0, 0]
        wb_evs = [0] * NE
        wrow_sps = [0] * NE

        def emit_wrow(e):
            if e < 8:
                EW("sp", DV, wT_ev)
                if e > 0:
                    EW("sp", AC, wb_evs[e - 1])
                wrow_sps[e] = DMA("sp", WRO, r_g(0, T), w_eT[e:e + 1, :])

        def emit_wb(e, war_dv=0):
            wb_ev_chunks = 0
            if war_dv:
                EW("pe", DV, war_dv)
            for ci, (c0, cn) in enumerate(CH_T):
                if e < 8:
                    EW("pe", WRO, wrow_sps[e])
                else:
                    EW("pe", AC, sw_ev)
                wsrc = r_g(c0, cn) if e < 8 else r_pmk(c0, cn)
                bank_wait(4)
                vb = MM(PSA[4][:, 0:cn], ones_r(0), wsrc, True, True, ctr=PE)
                EW("ac", PE, vb)
                wb_ev_chunks = ACT(abf(F_WB + c0, cn), PSA[4][:, 0:cn], AF.Copy, ctr=AC)
                bank_set(4, AC, wb_ev_chunks)
            wb_evs[e] = wb_ev_chunks

        emit_wrow(0)
        emit_wb(0)

        p7_state = {"ost": [0, 0], "out_prev": 0}

        def emit_p7_chunk(no, ready_dv):
            c0, cn = CH_T[no]
            EW("pe", DV, ready_dv)
            EW("ac", DV, ready_dv)
            if no > 0:
                EW("gp", DV, p7_state["out_prev"])
            else:
                EW("gp", DV, ready_dv)
                EW("gp", SPL, spill_vs[3])
            xm_ld = DMA("gp", GX, abf(F_XM, 4096).rearrange("p (k t) -> p k t", k=KT),
                        xmid_d[:, c0:c0 + cn].rearrange("(k p) t -> p k t", p=128))
            vms = ln_stats_chunk(lambda mo, a, b: abf(F_MOE + mo * 2048 + a, b),
                                 c0, cn, X2B)
            ln_finish_chunk(cn, vms)
            EW("dv", GX, xm_ld)
            out_dv_prev = 0
            for mo in range(MO):
                osl = (no * MO + mo) % 2
                if p7_state["ost"][osl]:
                    EW("dv", OS[osl], p7_state["ost"][osl])
                moff = F_MOE + mo * 2048 + c0
                TT("dv", abf(H_T2, cn), abf(moff, cn), abf(H_RB, cn), OP.mult)
                TT("dv", abf(H_T2, cn), abf(H_T2, cn), abf(H_MRB, cn), OP.subtract)
                vout = TT("dv", afp(osl * 512, cn), abf(F_XM + mo * 512, cn),
                          abf(H_T2, cn), OP.add, ctr=DV)
                out_dv_prev = vout
                EW("sp", DV, vout)
                p7_state["ost"][osl] = DMA("sp", OS[osl],
                                           out_d[mo * 128:(mo + 1) * 128, c0:c0 + cn],
                                           afp(osl * 512, cn))
            p7_state["out_prev"] = out_dv_prev
            ln_war[0] = out_dv_prev

        for e in range(NE):
            wb_ev = wb_evs[e]
            h_done_dv = 0
            h_p9 = 0
            h10_ci = [0] * 4
            for j in range(PJ):
                idx = e * PJ + j
                slot = GUSLOT[idx % 2]
                EW("pe", GUS[idx % 2], gu_ld[idx])
                for ci in range(4):
                    bank_wait(ci)
                vg = [0] * 4
                for kt in range(KT):
                    for ci, (c0, cn) in enumerate(CH_T):
                        v = MM(PSA[ci][:, 0:cn], abf(slot + kt * 256, 128),
                               xn2(kt, c0, cn), kt == 0, kt == KT - 1,
                               ctr=PE if kt == KT - 1 else None)
                        if kt == KT - 1:
                            vg[ci] = v
                for ci in range(4):
                    bank_wait(4 + ci)
                vu = [0] * 4
                for kt in range(KT):
                    for ci, (c0, cn) in enumerate(CH_T):
                        v = MM(PSA[4 + ci][:, 0:cn], abf(slot + kt * 256 + 128, 128),
                               xn2(kt, c0, cn), kt == 0, kt == KT - 1,
                               ctr=PE if kt == KT - 1 else None)
                        if kt == KT - 1:
                            vu[ci] = v
                gu_rel[idx] = vu[-1]
                emit_db_tile(e, j)
                if idx + 2 < NE * PJ:
                    e2, j2 = divmod(idx + 2, PJ)
                    emit_gu_load(e2, j2)
                vus = [0] * 4
                for ci, (c0, cn) in enumerate(CH_T):
                    hs = F_XM + ci * 512
                    EW("ac", PE, vg[ci])
                    if hs_war_dv[ci]:
                        EW("ac", DV, hs_war_dv[ci])
                    vhs = ACT(abf(hs, cn), PSA[ci][:, 0:cn], AF.Silu, ctr=AC)
                    bank_set(ci, AC, vhs)
                for ci, (c0, cn) in enumerate(CH_T):
                    us = F_XM + 2048 + ci * 512
                    EW("ac", PE, vu[ci])
                    vus[ci] = ACT(abf(us, cn), PSA[4 + ci][:, 0:cn], AF.Copy, ctr=AC)
                    bank_set(4 + ci, AC, vus[ci])
                for ci, (c0, cn) in enumerate(CH_T):
                    hs = F_XM + ci * 512
                    us = F_XM + 2048 + ci * 512
                    EW("dv", AC, vus[ci])
                    if j == 0 and ci == 0:
                        EW("dv", AC, wb_ev)
                    hoff = F_H + j * 2048 + c0
                    hs_war_dv[ci] = TT("dv", abf(hoff, cn), abf(hs, cn),
                                       abf(us, cn), OP.mult, ctr=DV)
                    h_done_dv = TT("dv", abf(hoff, cn), abf(hoff, cn),
                                   abf(F_WB + c0, cn), OP.mult, ctr=DV)
                    if j == PJ - 1:
                        h10_ci[ci] = h_done_dv
                if j == PJ - 2:
                    h_p9 = h_done_dv
            EW("pe", GDN, db_ld[e])
            EW("pe", DV, h_p9)
            if e + 1 < NE:
                emit_wrow(e + 1)
            if e < NE - 1 or (stop is not None and str(stop).startswith("p6")):
                for mo in range(MO):
                    for ci, (c0, cn) in enumerate(CH_T):
                        bk = (mo * 4 + ci) % 4
                        bank_wait(bk)
                        vd = 0
                        for kt in range(KI):
                            if mo == 0 and kt == KI - 1:
                                EW("pe", DV, h10_ci[ci])
                            vd = MM(PSA[bk][:, 0:cn],
                                    abf(F_DB + kt * 1024 + mo * 128, 128),
                                    abf(F_H + kt * 2048 + c0, cn), kt == 0,
                                    kt == KI - 1, ctr=PE if kt == KI - 1 else None)
                        if mo == MO - 1 and ci == 3:
                            db_rel[e] = vd
                        moff = F_MOE + mo * 2048 + c0
                        eo = H_T2 if bk % 2 == 0 else H_RB
                        if e == 0:
                            EW("dv", PE, vd)
                            vev = CPY("dv", abf(moff, cn), PSA[bk][:, 0:cn], ctr=DV)
                            bank_set(bk, DV, vev)
                            moe_add_dv = vev
                        else:
                            EW("ac", PE, vd)
                            if eo_war[bk % 2]:
                                EW("ac", DV, eo_war[bk % 2])
                            vev = ACT(abf(eo, cn), PSA[bk][:, 0:cn], AF.Copy,
                                      ctr=AC)
                            bank_set(bk, AC, vev)
                            EW("dv", AC, vev)
                            moe_add_dv = TT("dv", abf(moff, cn), abf(moff, cn),
                                            abf(eo, cn), OP.add, ctr=DV)
                            eo_war[bk % 2] = moe_add_dv
                    if mo == 1 and e + 1 < NE:
                        emit_wb(e + 1, war_dv=h10_ci[3])
            else:
                madds = [0] * 4
                for ci, (c0, cn) in enumerate(CH_T):
                    madd = 0
                    for mo in range(MO):
                        bk = mo % 4
                        bank_wait(bk)
                        vd = 0
                        for kt in range(KI):
                            if mo == 0 and kt == KI - 1:
                                EW("pe", DV, h10_ci[ci])
                            vd = MM(PSA[bk][:, 0:cn],
                                    abf(F_DB + kt * 1024 + mo * 128, 128),
                                    abf(F_H + kt * 2048 + c0, cn), kt == 0,
                                    kt == KI - 1, ctr=PE if kt == KI - 1 else None)
                        if ci == 3 and mo == MO - 1:
                            db_rel[e] = vd
                        moff = F_MOE + mo * 2048 + c0
                        eo = 51202 + (bk % 2) * 512
                        EW("ac", PE, vd)
                        if eo_war[bk % 2]:
                            EW("ac", DV, eo_war[bk % 2])
                        vev = ACT(abf(eo, cn), PSA[bk][:, 0:cn], AF.Copy, ctr=AC)
                        bank_set(bk, AC, vev)
                        EW("dv", AC, vev)
                        madd = TT("dv", abf(moff, cn), abf(moff, cn),
                                  abf(eo, cn), OP.add, ctr=DV)
                        eo_war[bk % 2] = madd
                    madds[ci] = madd
                    if ci >= 1:
                        emit_p7_chunk(ci - 1, madds[ci - 1])
                emit_p7_chunk(3, madds[3])
            if stop == "p6e%d" % e:
                return finish()

        if stop == "p6":
            return finish()
        EW("sp", OS[0], OS[0].n)
        EW("sp", OS[1], OS[1].n)
        return finish()


_CACHE = {}


def _prep(inputs):
    f32 = np.float32
    x = np.asarray(inputs["L0_attn_x"], f32)
    xT = np.ascontiguousarray(x.transpose(0, 2, 1))

    shared = {
        "w_in_T": np.ascontiguousarray(np.asarray(inputs["L0_attn_in_proj_w"], f32).T),
        "conv_w": np.ascontiguousarray(
            np.asarray(inputs["L0_attn_conv_w"], f32)[:, 0, :]
            .reshape(16, 128, 3).transpose(1, 0, 2).reshape(128, 48)),
        "conv_b": np.ascontiguousarray(
            np.asarray(inputs["L0_attn_conv_b"], f32).reshape(16, 128).T),
        "d_param": np.ascontiguousarray(
            np.asarray(inputs["L0_attn_D_param"], f32).reshape(16, 128).T),
        "dt_b": np.ascontiguousarray(
            np.asarray(inputs["L0_attn_dt_proj_b"], f32).reshape(16, 128).T),
        "xpw": np.ascontiguousarray(np.asarray(inputs["L0_attn_x_proj_w"], f32)[0:64]),
        "dtw": np.ascontiguousarray(np.asarray(inputs["L0_attn_dt_proj_w"], f32)),
        "w_out_T": np.ascontiguousarray(np.asarray(inputs["L0_attn_out_proj_w"], f32).T),
        "gw_T": np.ascontiguousarray(np.asarray(inputs["L0_ffn_gate_w"], f32).T),
        "seg_T": np.ascontiguousarray(
            np.asarray(inputs["L0_ffn_shared_expert_gate_w"], f32).T),
        "ident": np.eye(128, dtype=f32),
        "ones_row": np.ones((65, 128), f32),
        "ones_col": np.ones((128, 1), f32),
    }
    egu = np.asarray(inputs["L0_ffn_experts_gate_up"], f32)
    edn = np.asarray(inputs["L0_ffn_experts_down"], f32)
    sg = np.asarray(inputs["L0_ffn_shared_gate_w"], f32)
    su = np.asarray(inputs["L0_ffn_shared_up_w"], f32)
    sd = np.asarray(inputs["L0_ffn_shared_down_w"], f32)
    gu_all = np.empty((NE, D, GU), f32)
    gu_all[0:8] = egu.transpose(0, 2, 1)
    gu_all[8] = np.concatenate([sg[0:IH], su[0:IH]], axis=0).T
    gu_all[9] = np.concatenate([sg[IH:], su[IH:]], axis=0).T
    perm = np.empty(GU, np.int64)
    for j in range(PJ):
        perm[j * 256: j * 256 + 128] = np.arange(j * 128, (j + 1) * 128)
        perm[j * 256 + 128: (j + 1) * 256] = IH + np.arange(j * 128, (j + 1) * 128)
    gu_all = gu_all[:, :, perm]
    down_all = np.empty((NE, IH, D), f32)
    down_all[0:8] = edn.transpose(0, 2, 1)
    down_all[8] = sd[:, 0:IH].T
    down_all[9] = sd[:, IH:].T
    shared["gu_all"] = np.ascontiguousarray(gu_all)
    shared["down_all"] = np.ascontiguousarray(down_all)

    in_maps = []
    for core in range(8):
        b, q = divmod(core, 4)
        t0 = q * T
        lo, hi = t0 - (W + 1), t0 + T + 1
        xwin = np.zeros((D, XW), f32)
        s, e2 = max(lo, 0), min(hi, SEQ)
        xwin[:, s - lo: s - lo + (e2 - s)] = xT[b][:, s:e2]
        pmask = np.ones((1, TW), f32)
        if q == 0:
            pmask[:, 0:W] = 0.0
        m = dict(shared)
        m["xw"] = xwin
        m["pmask"] = pmask
        in_maps.append(m)
    return in_maps


def kernel(**inputs):
    in_maps = _prep(inputs)
    if "nc" not in _CACHE:
        _CACHE["nc"] = _build()
    res = run_bass_kernel_spmd(_CACHE["nc"], in_maps, core_ids=list(range(8)))

    out = np.empty((B, SEQ, D), np.float32)
    for core in range(8):
        b, q = divmod(core, 4)
        out[b, q * T:(q + 1) * T, :] = res.results[core]["out"].T
    return out


# version 21
# speedup vs baseline: 1.0115x; 1.0115x over previous
import numpy as np
import concourse.bass as bass
import concourse.mybir as mybir
from concourse.bass_utils import run_bass_kernel_spmd

F32 = mybir.dt.float32
BF16 = mybir.dt.bfloat16
AF = mybir.ActivationFunctionType
OP = mybir.AluOpType
AX = mybir.AxisListType

B, SEQ, D = 2, 8192, 1024
T = 2048
W = 256
XW = W + 1 + T + 1
TW = W + T
KT = D // 128
DI = 2048
CT = DI // 128
NE = 10
IH = 1408
KI = IH // 128
GU = 2816
PJ = 11
MO = 8
EPS = 1e-5

CH_X = [(0, 512), (512, 512), (1024, 512), (1536, 512), (2048, 258)]
CH_G = [(0, 512), (512, 512), (1024, 512), (1536, 512), (2048, 256)]
CH_T = [(0, 512), (512, 512), (1024, 512), (1536, 512)]

A_XSC = 0
A_OMG = 36864
A_CT1 = 39168
A_CT2 = 41472
A_XSP = 43776
A_ZS = 46082
A_WA = 50178
A_WO = 52226
A_X4 = 56324
AO_EVEN = 46082
AO_ODD = 36864
X4B = 40960
X2A = 43776
X2B = 50178
F_MOE = 0
F_H = 16384
F_DB = 38912
F_WG = 50176
F_WU = 52224
F_WB = 54272
F_XM = 56320
H_RB = 60424
H_MRB = 60936
H_T2 = 61448
ABF_N = 61960
AF_M = 1280
AF_S = 1792
AF_T = 2304
AF32_N = 2816
RG = 0
RPM = 2304
RMEAN = 4608
RMS = 5120
ROWS_N = 5632


def _build(stop=None, dbg=False):
    nc = bass.Bass("TRN2", target_bir_lowering=False)
    dr = {}
    for nm, sh in [
        ("xw", [D, XW]), ("pmask", [1, TW]),
        ("w_in_T", [D, 2 * DI]), ("conv_w", [128, 48]), ("conv_b", [128, 16]),
        ("d_param", [128, 16]), ("dt_b", [128, 16]), ("xpw", [64, DI]),
        ("dtw", [DI, 64]), ("w_out_T", [DI, D]), ("gw_T", [D, 8]),
        ("seg_T", [D, 1]), ("gu_all", [NE, D, GU]), ("down_all", [NE, IH, D]),
        ("ident", [128, 128]), ("ones_row", [65, 128]), ("ones_col", [128, 1]),
    ]:
        dr[nm] = nc.dram_tensor(nm, sh, F32, kind="ExternalInput").ap()
    out_d = nc.dram_tensor("out", [D, T], F32, kind="ExternalOutput").ap()
    xmid_d = nc.dram_tensor("xmid_hbm", [D, T], BF16,
                            kind="ExternalOutput" if dbg else "Internal").ap()
    dbg_d = {}
    if dbg:
        for nm, sh, dt in [("dbg_abf", [128, ABF_N], BF16),
                           ("dbg_xn", [128, KT * XW], BF16),
                           ("dbg_rows", [128, ROWS_N], F32),
                           ("dbg_afp", [128, AF32_N], F32),
                           ("dbg_sm", [128, 24], F32),
                           ("dbg_tok", [128, 128], F32),
                           ("dbg_scs", [128, 128], F32),
                           ("dbg_wsc", [128, 128], F32),
                           ("dbg_wet", [8, T], F32)]:
            dbg_d[nm] = nc.dram_tensor(nm, sh, dt, kind="ExternalOutput").ap()

    import contextlib
    with contextlib.ExitStack() as ctx:
        sb = lambda n, s, t: ctx.enter_context(nc.sbuf_tensor(n, s, t))
        ps = lambda n, s: ctx.enter_context(nc.psum_tensor(n, s, F32))

        ABF = sb("ABF", [128, ABF_N], BF16)
        AFP = sb("AFP", [128, AF32_N], F32)
        XN = sb("XN", [128, KT * XW], BF16)
        ROWS = sb("ROWS", [128, ROWS_N], F32)
        ident_f = sb("ident_f", [128, 128], F32)
        onesr_f = sb("onesr_f", [65, 128], F32)
        onesc_bf = sb("onesc_bf", [128, 1], BF16)
        conv_f = sb("conv_f", [128, 48], F32)
        cb_f = sb("cb_f", [128, 16], F32)
        cb_bf = sb("cb_bf", [128, 16], BF16)
        dpar_f = sb("dpar_f", [128, 16], F32)
        dtb_bf = sb("dtb_bf", [128, 16], BF16)
        gw_bf = sb("gw_bf", [128, KT * 8], BF16)
        seg_bf = sb("seg_bf", [128, KT], BF16)
        sm_f = sb("sm_f", [128, 24], F32)
        sm_bf = sb("sm_bf", [128, 18], BF16)
        tok_sc = sb("tok_sc", [128, 128], F32)
        sc_scr = sb("sc_scr", [128, 128], F32)
        w_sc = sb("w_sc", [128, 128], F32)
        st8 = sb("st8", [128, 32], F32)
        lgch = sb("lgch", [8, 512], F32)
        w_eT = sb("w_eT", [8, T], F32)

        PSA = [ps("PSA%d" % i, [128, 512]) for i in range(8)]

        sem_names = ["PE", "AC", "DV", "GPC", "GX", "WA0", "WA1", "WO0", "WO1",
                     "GU0", "GU1", "GU2", "GDN", "SPC", "SPL", "WRO", "OS0", "OS1"]
        sems = {s: ctx.enter_context(nc.semaphore(s)) for s in sem_names}

        class Ctr:
            __slots__ = ("h", "n")
            def __init__(self, h): self.h = h; self.n = 0
        C = {s: Ctr(sems[s]) for s in sem_names}
        PE, AC, DV = C["PE"], C["AC"], C["DV"]
        GPC, GX, GDN = C["GPC"], C["GX"], C["GDN"]
        WA = [C["WA0"], C["WA1"]]; WO = [C["WO0"], C["WO1"]]
        GUS = [C["GU0"], C["GU1"], C["GU2"]]
        SPC, SPL, WRO = C["SPC"], C["SPL"], C["WRO"]
        OS = [C["OS0"], C["OS1"]]

        ops = {"sp": [], "gp": [], "pe": [], "ac": [], "dv": []}

        def EW(q, ctr, v):
            if v > 0:
                ops[q].append(lambda e, h=ctr.h, vv=v: e.wait_ge(h, vv))

        def EI(q, fn, ctr=None, k=1):
            if ctr is None:
                ops[q].append(fn)
                return None
            ctr.n += k
            ops[q].append(lambda e, f=fn, h=ctr.h, kk=k: f(e).then_inc(h, kk))
            return ctr.n

        def DMA(q, ctr, out_ap, in_ap):
            return EI(q, lambda e, o=out_ap, i=in_ap: e.dma_start(out=o, in_=i), ctr, 16)

        def MM(psum_ap, lhsT_ap, rhs_ap, start, stop, ctr=None):
            return EI("pe", lambda e, a=psum_ap, b=lhsT_ap, c=rhs_ap, s=start, t=stop:
                      e.matmul(a, b, c, start=s, stop=t), ctr)

        def ACT(out_ap, in_ap, func, bias=0.0, scale=1.0, ctr=None):
            return EI("ac", lambda e, o=out_ap, i=in_ap, f=func, bi=bias, sc=scale:
                      e.activation(o, i, f, bias=bi, scale=sc), ctr)

        def TT(q, out_ap, a_ap, b_ap, op, ctr=None):
            return EI(q, lambda e, o=out_ap, a=a_ap, b=b_ap, p=op:
                      e.tensor_tensor(o, a, b, p), ctr)

        def TS(q, out_ap, in_ap, s1, s2, op0, op1, ctr=None):
            return EI(q, lambda e, o=out_ap, i=in_ap, a=s1, b=s2, p0=op0, p1=op1:
                      e.tensor_scalar(o, i, a, b, p0, p1), ctr)

        def STT(q, out_ap, in0_ap, scal, in1_ap, op0, op1, ctr=None):
            return EI(q, lambda e, o=out_ap, i=in0_ap, s=scal, j=in1_ap, p0=op0, p1=op1:
                      e.scalar_tensor_tensor(o, i, s, j, p0, p1), ctr)

        def CPY(q, out_ap, in_ap, ctr=None):
            return EI(q, lambda e, o=out_ap, i=in_ap: e.tensor_copy(o, i), ctr)

        def abf(off, n):  return ABF[:, off:off + n]
        def afp(off, n):  return AFP[:, off:off + n]
        def xnv(kt, c0, n):  return XN[:, kt * XW + c0: kt * XW + c0 + n]
        def xsc(ct, c0, n):  return abf(A_XSC + ct * TW + c0, n)
        def xwv(kt, c0, n):  return abf(kt * XW + c0, n)

        def ones_r(base):  return onesr_f[base:base + 1, :]

        xpw_bf = ABF[0:64, 54274:54274 + DI]
        dtw_bf = ABF[:, A_WO:A_WO + 1024]

        r_g    = lambda c0, n: ROWS[0:1, RG + c0: RG + c0 + n]
        r_pmk  = lambda c0, n: ROWS[0:1, RPM + c0: RPM + c0 + n]
        r_mean = lambda n: ROWS[0:1, RMEAN: RMEAN + n]
        r_ms   = lambda n: ROWS[0:1, RMS: RMS + n]

        bank_ev = [None] * 8
        def bank_wait(i):
            st = bank_ev[i]
            if st is not None:
                EW("pe", st[0], st[1])
                bank_ev[i] = None
        def bank_set(i, ctr, val):
            bank_ev[i] = (ctr, val)

        x2_ev = [0, 0]

        def ln_stats_chunk(ap_fn, c0, cn, x2off):
            x2slots = [x2off, x2off + 512]
            bank_wait(6); bank_wait(7)
            for mo in range(MO):
                sl = x2slots[mo % 2]
                if x2_ev[mo % 2]:
                    EW("ac", PE, x2_ev[mo % 2])
                vac = ACT(abf(sl, cn), ap_fn(mo, c0, cn), AF.Square, ctr=AC)
                MM(PSA[6][0:1, 0:cn], onesc_bf[:, :], ap_fn(mo, c0, cn),
                   mo == 0, mo == MO - 1)
                EW("pe", AC, vac)
                vpe = MM(PSA[7][0:1, 0:cn], onesc_bf[:, :], abf(sl, cn), mo == 0,
                         mo == MO - 1, ctr=PE)
                x2_ev[mo % 2] = vpe
            EW("ac", PE, PE.n)
            ACT(r_mean(cn), PSA[6][0:1, 0:cn], AF.Copy, scale=1.0 / D)
            vms = ACT(r_ms(cn), PSA[7][0:1, 0:cn], AF.Copy, scale=1.0 / D, ctr=AC)
            bank_set(6, AC, vms); bank_set(7, AC, vms)
            return vms

        ln_war = [0]

        def ln_finish_chunk(cn, vms):
            EW("pe", AC, vms)
            if ln_war[0]:
                EW("pe", DV, ln_war[0])
            bank_wait(4); bank_wait(5)
            MM(PSA[4][:, 0:cn], ones_r(0), r_mean(cn), True, True)
            vb = MM(PSA[5][:, 0:cn], ones_r(0), r_ms(cn), True, True, ctr=PE)
            EW("ac", PE, vb)
            ACT(afp(AF_M, cn), PSA[4][:, 0:cn], AF.Copy)
            vev = ACT(afp(AF_S, cn), PSA[5][:, 0:cn], AF.Copy, ctr=AC)
            bank_set(4, AC, vev); bank_set(5, AC, vev)
            EW("dv", AC, vev)
            TT("dv", afp(AF_T, cn), afp(AF_M, cn), afp(AF_M, cn), OP.mult)
            TT("dv", afp(AF_S, cn), afp(AF_S, cn), afp(AF_T, cn), OP.subtract)
            vvar = EI("dv", lambda e, o=afp(AF_S, cn), i=afp(AF_S, cn):
                      e.tensor_scalar_add(o, i, EPS), DV)
            EW("ac", DV, vvar)
            vstd = ACT(afp(AF_S, cn), afp(AF_S, cn), AF.Sqrt, ctr=AC)
            EW("dv", AC, vstd)
            EI("dv", lambda e, o=afp(AF_S, cn), i=afp(AF_S, cn): e.reciprocal(o, i))
            TT("dv", abf(H_MRB, cn), afp(AF_M, cn), afp(AF_S, cn), OP.mult, ctr=DV)
            vfin = CPY("dv", abf(H_RB, cn), afp(AF_S, cn), ctr=DV)
            EW("dv", DV, vfin)
            return vfin

        def finish():
            if dbg:
                for c in [PE, AC, DV]:
                    EW("sp", c, c.n)
                for dst, src in [
                    (dbg_d["dbg_abf"], ABF[:, :]), (dbg_d["dbg_xn"], XN[:, :]),
                    (dbg_d["dbg_rows"], ROWS[:, :]), (dbg_d["dbg_afp"], AFP[:, :]),
                    (dbg_d["dbg_sm"], sm_f[:, :]), (dbg_d["dbg_tok"], tok_sc[:, :]),
                    (dbg_d["dbg_scs"], sc_scr[:, :]), (dbg_d["dbg_wsc"], w_sc[:, :]),
                    (dbg_d["dbg_wet"], w_eT[:, :]),
                ]:
                    DMA("sp", SPL, dst, src)
            EW("sp", SPL, SPL.n)
            blk = ctx.enter_context(nc.Block())
            for q, attr in [("sp", "sync"), ("gp", "gpsimd"), ("pe", "tensor"),
                            ("ac", "scalar"), ("dv", "vector")]:
                def section(eng, _q=q):
                    for f in ops[_q]:
                        f(eng)
                getattr(blk, attr)(section)
            return nc

        for (o, i) in [(ident_f[:, :], dr["ident"]), (onesr_f[:, :], dr["ones_row"]),
                       (conv_f[:, :], dr["conv_w"]), (cb_f[:, :], dr["conv_b"]),
                       (dpar_f[:, :], dr["d_param"]), (r_pmk(0, TW), dr["pmask"])]:
            DMA("sp", SPC, o, i)
        v_sp_consts = SPC.n

        DMA("gp", GPC, onesc_bf[:, :], dr["ones_col"])
        DMA("gp", GPC, cb_bf[:, :], dr["conv_b"])
        DMA("gp", GPC, dtb_bf[:, :], dr["dt_b"])
        DMA("gp", GPC, gw_bf.ap().rearrange("p (k e) -> p k e", k=KT),
            dr["gw_T"].rearrange("(k p) e -> p k e", p=128))
        EI("gp", lambda e, o=seg_bf.ap().rearrange("p (k o) -> p k o", k=KT),
           i=dr["seg_T"].rearrange("(k p) o -> p k o", p=128):
           e.dma_start(out=o, in_=i, allow_slow_non_contiguous=True), GPC, 16)
        v_gp_small = GPC.n

        wa_ld = [0] * 32
        wa_rel = [0] * 32
        def emit_wa_load(k):
            ct = k % 16
            col0 = (0 if k < 16 else DI) + ct * 128
            if k >= 2:
                EW("gp", PE, wa_rel[k - 2])
            slot = A_WA + (k % 2) * 1024
            wa_ld[k] = DMA("gp", WA[k % 2],
                           abf(slot, 1024).rearrange("p (k m) -> p k m", k=KT),
                           dr["w_in_T"][:, col0:col0 + 128]
                           .rearrange("(k p) m -> p k m", p=128))

        v_xw = [0] * len(CH_X)
        def emit_xw(ci):
            c0, cn = CH_X[ci]
            v_xw[ci] = DMA("gp", GX,
                           ABF.ap()[:, 0:KT * XW]
                           .rearrange("p (k t) -> p k t", k=KT)[:, :, c0:c0 + cn],
                           dr["xw"][:, c0:c0 + cn]
                           .rearrange("(k p) t -> p k t", p=128))
        emit_xw(0); emit_xw(1); emit_xw(2)
        emit_wa_load(0)
        emit_xw(3); emit_xw(4)
        emit_wa_load(1)
        DMA("gp", GPC, xpw_bf[:, :], dr["xpw"])
        DMA("gp", GPC, dtw_bf.rearrange("p (t r) -> p t r", t=16),
            dr["dtw"].rearrange("(t p) r -> p t r", p=128))
        v_gp_consts = GPC.n

        EW("pe", GPC, v_gp_small)
        EW("pe", SPC, v_sp_consts)
        v_ln1_done_dv = 0
        v_ln1_ci = [0] * len(CH_X)
        for ci, (c0, cn) in enumerate(CH_X):
            EW("ac", GX, v_xw[ci])
            EW("pe", GX, v_xw[ci])
            vms = ln_stats_chunk(lambda mo, a, b: xwv(mo, a, b), c0, cn, X2A)
            ln_finish_chunk(cn, vms)
            for kt in range(KT):
                TT("dv", abf(H_T2, cn), xwv(kt, c0, cn), abf(H_RB, cn), OP.mult)
                v_ln1_done_dv = TT("dv", xnv(kt, c0, cn), abf(H_T2, cn),
                                   abf(H_MRB, cn), OP.subtract, ctr=DV)
            ln_war[0] = v_ln1_done_dv
            v_ln1_ci[ci] = v_ln1_done_dv

        if stop == "p0":
            return finish()
        conv_done = [0] * CT
        EW("dv", SPC, v_sp_consts)
        for ct in range(CT):
            slot = A_WA + (ct % 2) * 1024
            EW("pe", WA[ct % 2], wa_ld[ct])
            for ci in range(len(CH_X)):
                bank_wait(ci)
            vstop = [0] * len(CH_X)
            for kt in range(KT):
                for ci, (c0, cn) in enumerate(CH_X):
                    if ct == 0 and kt == 0:
                        EW("pe", DV, v_ln1_ci[ci])
                    v = MM(PSA[ci][:, 0:cn], abf(slot + kt * 128, 128),
                           xnv(kt, c0, cn), kt == 0, kt == KT - 1,
                           ctr=PE if kt == KT - 1 else None)
                    if kt == KT - 1:
                        vstop[ci] = v
            wa_rel[ct] = vstop[-1]
            ev_last = 0
            for ci, (c0, cn) in enumerate(CH_X):
                EW("ac", PE, vstop[ci])
                if ci == 0 and ct > 0:
                    EW("ac", DV, conv_done[ct - 1])
                ev_last = ACT(abf(A_XSP + c0, cn), PSA[ci][:, 0:cn], AF.Copy, ctr=AC)
                bank_set(ci, AC, ev_last)
            if ct + 2 < 32:
                emit_wa_load(ct + 2)
            EW("dv", AC, ev_last)
            cw = lambda k: conv_f[:, ct * 3 + k: ct * 3 + k + 1]
            TS("dv", abf(A_CT1, TW), abf(A_XSP, TW), cw(0), cb_f[:, ct:ct + 1],
               OP.mult, OP.add)
            STT("dv", abf(A_CT2, TW), abf(A_XSP + 1, TW), cw(1), abf(A_CT1, TW),
                OP.mult, OP.add)
            conv_done[ct] = STT("dv", xsc(ct, 0, TW), abf(A_XSP + 2, TW), cw(2),
                                abf(A_CT2, TW), OP.mult, OP.add, ctr=DV)

        EW("pe", GPC, v_gp_consts)
        bank_wait(3)
        for t in range(16):
            MM(PSA[3][0:64, 0:1], dtw_bf[:, t * 64:(t + 1) * 64], onesc_bf[:, :],
               t == 0, t == 15, ctr=PE if t == 15 else None)
        EW("ac", PE, PE.n)
        vwm_ev = ACT(sm_f[0:64, 0:1], PSA[3][0:64, 0:1], AF.Copy, scale=1.0 / DI, ctr=AC)
        EW("dv", AC, vwm_ev)
        vwmbf = CPY("dv", sm_bf[0:64, 16:17], sm_f[0:64, 0:1], ctr=DV)
        EW("pe", DV, vwmbf)
        for ct in range(CT):
            MM(PSA[3][:, 4 + ct:5 + ct], xpw_bf[0:64, ct * 128:(ct + 1) * 128],
               sm_bf[0:64, 16:17], True, True, ctr=PE if ct == CT - 1 else None)
        EW("ac", PE, PE.n)
        vv_ev = ACT(sm_f[:, 4:20], PSA[3][:, 4:20], AF.Copy, ctr=AC)
        bank_set(3, AC, vv_ev)
        EW("dv", AC, vv_ev)
        vvbf = CPY("dv", sm_bf[:, 0:16], sm_f[:, 4:20], ctr=DV)
        EW("pe", DV, vvbf)
        bank_wait(6)
        for t in range(16):
            MM(PSA[6][0:1, 0:1], dtb_bf[:, t:t + 1], onesc_bf[:, :], t == 0, t == 15)
        for t in range(16):
            MM(PSA[6][0:1, 1:2], cb_bf[:, t:t + 1], sm_bf[:, t:t + 1], t == 0,
               t == 15, ctr=PE if t == 15 else None)
        EW("ac", PE, PE.n)
        ACT(sm_f[0:1, 20:21], PSA[6][0:1, 0:1], AF.Copy, scale=1.0 / DI)
        vbm1 = ACT(sm_f[0:1, 21:22], PSA[6][0:1, 1:2], AF.Copy, ctr=AC)
        bank_set(6, AC, vbm1)
        EW("dv", AC, vbm1)
        v_bm_dv = TT("dv", sm_f[0:1, 20:21], sm_f[0:1, 20:21], sm_f[0:1, 21:22],
                     OP.add, ctr=DV)

        if stop == "p0b":
            return finish()
        EW("pe", DV, conv_done[CT - 1])
        g_ev = 0
        EW("ac", DV, v_bm_dv)
        bank_wait(6)
        for (c0, cn) in CH_G:
            if g_ev:
                EW("pe", AC, g_ev)
            for ct in range(CT):
                MM(PSA[6][0:1, 0:cn], sm_bf[:, ct:ct + 1], xsc(ct, c0, cn),
                   ct == 0, ct == CT - 1, ctr=PE if ct == CT - 1 else None)
            EW("ac", PE, PE.n)
            g_ev = ACT(r_g(c0, cn), PSA[6][0:1, 0:cn], AF.Sigmoid,
                       bias=sm_f[0:1, 20:21], ctr=AC)
        bank_set(6, AC, g_ev)

        if stop == "p1":
            return finish()
        EW("dv", AC, g_ev)
        EW("dv", SPC, v_sp_consts)
        TT("dv", AFP[0:1, 0:TW], r_g(0, TW), r_pmk(0, TW), OP.mult)
        vomg = TT("dv", r_pmk(0, TW), r_pmk(0, TW), AFP[0:1, 0:TW],
                  OP.subtract, ctr=DV)
        EW("pe", DV, vomg)
        EW("pe", DV, ln_war[0])
        bc_ev_dv = 0
        bc_ev_ac = 0
        for (c0, cn) in CH_G:
            bank_wait(4); bank_wait(5)
            MM(PSA[4][:, 0:cn], ones_r(0), r_g(c0, cn), True, True)
            vb = MM(PSA[5][:, 0:cn], ones_r(0), r_pmk(c0, cn), True, True, ctr=PE)
            EW("dv", PE, vb)
            bc_ev_dv = CPY("dv", afp(c0, cn), PSA[4][:, 0:cn], ctr=DV)
            bank_set(4, DV, bc_ev_dv)
            EW("ac", PE, vb)
            bc_ev_ac = ACT(abf(A_OMG + c0, cn), PSA[5][:, 0:cn], AF.Copy, ctr=AC)
            bank_set(5, AC, bc_ev_ac)

        if stop == "p2":
            return finish()
        zs_ready = [0] * CT
        gate_done = [0] * CT
        for ct in range(CT):
            k = 16 + ct
            slot = A_WA + (k % 2) * 1024
            zslot = A_ZS + (ct % 2) * 2048
            EW("pe", WA[k % 2], wa_ld[k])
            for ci in range(4):
                bank_wait(ci)
            vstop = [0] * 4
            for kt in range(KT):
                for ci, (c0, cn) in enumerate(CH_T):
                    v = MM(PSA[ci][:, 0:cn], abf(slot + kt * 128, 128),
                           xnv(kt, 257 + c0, cn), kt == 0, kt == KT - 1,
                           ctr=PE if kt == KT - 1 else None)
                    if kt == KT - 1:
                        vstop[ci] = v
            wa_rel[k] = vstop[-1]
            for ci, (c0, cn) in enumerate(CH_T):
                EW("ac", PE, vstop[ci])
                if ct >= 2 and ci == 0:
                    EW("ac", DV, gate_done[ct - 2])
                vz = ACT(abf(zslot + c0, cn), PSA[ci][:, 0:cn], AF.Silu, ctr=AC)
                bank_set(ci, AC, vz)
            zs_ready[ct] = AC.n
            if k + 2 < 32:
                emit_wa_load(k + 2)
            if ct == 0:
                EW("dv", AC, bc_ev_ac)
                EW("dv", DV, bc_ev_dv)
            TT("dv", abf(A_CT1, TW), xsc(ct, 0, TW), abf(A_OMG, TW), OP.mult)
            EI("dv", lambda e, o=abf(A_CT2, TW), g=afp(0, TW), b=abf(A_CT1, TW):
               e.tensor_tensor_scan(o, g, b, 0.0, OP.mult, OP.add))
            EW("dv", AC, zs_ready[ct])
            TT("dv", abf(A_CT2 + W, T), abf(A_CT2 + W, T), abf(zslot, T), OP.mult)
            gate_done[ct] = STT("dv", xsc(ct, W, T), xsc(ct, W, T),
                                dpar_f[:, ct:ct + 1], abf(A_CT2 + W, T),
                                OP.mult, OP.add, ctr=DV)
        v_gate_all = gate_done[CT - 1]

        if stop == "p3":
            return finish()
        wo_ld = {}
        wo_rel = {}
        def emit_wo_load(idx):
            mo = idx % MO
            if idx >= 2:
                EW("gp", PE, wo_rel[idx - 2])
            slot = A_WO + (idx % 2) * 2048
            wo_ld[idx] = DMA("gp", WO[idx % 2],
                             abf(slot, 2048).rearrange("p (k m) -> p k m", k=CT),
                             dr["w_out_T"][:, mo * 128:(mo + 1) * 128]
                             .rearrange("(k p) m -> p k m", p=128))
        emit_wo_load(0); emit_wo_load(1)

        ao_off = [AO_EVEN, AO_ODD]
        x4_off = [A_X4, X4B]
        spill_vs = [0, 0, 0, 0]
        vres_chunk = [0, 0, 0, 0]
        ao_stop = [[0] * MO for _ in range(4)]
        ao_ev = [0, 0, 0, 0]
        vx4s = [0, 0, 0, 0]

        def emit_outproj_chunk(no, stages=None, sno=-1):
            c0, cn = CH_T[no]
            if no >= 2:
                EW("gp", SPL, spill_vs[no - 2])
                EW("gp", DV, xn2_dones[no - 2])
            vx4s[no] = DMA("gp", GX,
                           abf(x4_off[no % 2], 4096).rearrange("p (k t) -> p k t", k=KT),
                           dr["xw"][:, 257 + c0: 257 + c0 + cn]
                           .rearrange("(k p) t -> p k t", p=128))
            for mo in range(MO):
                if stages is not None and mo % 2 == 0 and mo // 2 < len(stages):
                    pass
                idx = no * MO + mo
                EW("pe", WO[idx % 2], wo_ld[idx])
                if idx == 0:
                    EW("pe", DV, v_gate_all)
                slot = A_WO + (idx % 2) * 2048
                bank_wait(mo)
                vpe = 0
                for kt in range(CT):
                    vpe = MM(PSA[mo][:, 0:cn], abf(slot + kt * 128, 128),
                             xsc(kt, W + c0, cn), kt == 0, kt == CT - 1,
                             ctr=PE if kt == CT - 1 else None)
                wo_rel[idx] = vpe
                ao_stop[no][mo] = vpe
                if idx + 2 < 32:
                    emit_wo_load(idx + 2)
                EW("ac", PE, vpe)
                if mo == 0 and no >= 2:
                    EW("ac", DV, vres_chunk[no - 2])
                ao_ev[no] = ACT(abf(ao_off[no % 2] + mo * 512, cn), PSA[mo][:, 0:cn],
                                AF.Copy, ctr=AC)
                bank_set(mo, AC, ao_ev[no])
                if stages is not None and mo % 2 == 1 and mo // 2 < len(stages):
                    stages[mo // 2](sno)

        xn2_dones = [0, 0, 0, 0]
        lnst = {}

        def ln_stageA(no):
            c0, cn = CH_T[no]
            off = ao_off[no % 2]
            EW("pe", AC, ao_ev[no])
            lnst["vms"] = ln_stats_chunk(
                lambda mo, a, b: abf(off + mo * 512 + (a - c0), b), c0, cn, X2B)

        def ln_stageB(no):
            c0, cn = CH_T[no]
            off = ao_off[no % 2]
            ln_finish_chunk(cn, lnst["vms"])
            EW("dv", GX, vx4s[no])
            vres = 0
            for mo in range(MO):
                TT("dv", abf(H_T2, cn), abf(off + mo * 512, cn), abf(H_RB, cn), OP.mult)
                TT("dv", abf(H_T2, cn), abf(H_T2, cn), abf(H_MRB, cn), OP.subtract)
                vres = TT("dv", abf(x4_off[no % 2] + mo * 512, cn),
                          abf(x4_off[no % 2] + mo * 512, cn), abf(H_T2, cn),
                          OP.add, ctr=DV)
            ln_war[0] = vres
            vres_chunk[no] = vres
            EW("sp", DV, vres)
            spill_vs[no] = DMA("sp", SPL,
                               xmid_d[:, c0:c0 + cn].rearrange("(k p) t -> p k t", p=128),
                               abf(x4_off[no % 2], 4096).rearrange("p (k t) -> p k t", k=KT))

        def ln_stageC(no):
            c0, cn = CH_T[no]
            vres = vres_chunk[no]
            EW("ac", DV, vres)
            EW("pe", DV, vres)
            lnst["vms"] = ln_stats_chunk(
                lambda mo, a, b: abf(x4_off[no % 2] + mo * 512 + (a - c0), b),
                c0, cn, X2B)

        def ln_stageD(no):
            c0, cn = CH_T[no]
            ln_finish_chunk(cn, lnst["vms"])
            xn2_done = 0
            for kt in range(KT):
                TT("dv", abf(H_T2, cn), abf(x4_off[no % 2] + kt * 512, cn),
                   abf(H_RB, cn), OP.mult)
                xn2_done = TT("dv", xnv(kt, c0, cn), abf(H_T2, cn), abf(H_MRB, cn),
                              OP.subtract, ctr=DV)
            ln_war[0] = xn2_done
            xn2_dones[no] = xn2_done

        lgst = {"lg_ev": 0, "v_tp": 0}

        def emit_logits(no):
            c0, cn = CH_T[no]
            EW("pe", DV, xn2_dones[no])
            bank_wait(3)
            bank_wait(5)
            for kt in range(KT):
                MM(PSA[5][0:8, 0:cn], gw_bf[:, kt * 8:(kt + 1) * 8], xnv(kt, c0, cn),
                   kt == 0, kt == KT - 1, ctr=PE if kt == KT - 1 else None)
            EW("dv", PE, PE.n)
            lgst["lg_ev"] = CPY("dv", lgch[:, 0:cn], PSA[5][0:8, 0:cn], ctr=DV)
            bank_set(5, DV, lgst["lg_ev"])
            EW("pe", DV, lgst["lg_ev"])
            vtp = 0
            for i in range(4):
                tt = no * 4 + i
                vtp = EI("pe", lambda e, o=PSA[3][:, i * 8:(i + 1) * 8],
                         s=lgch[0:8, i * 128:(i + 1) * 128],
                         idn=ident_f[0:8, 0:8]:
                         e.transpose(o, s, idn), ctr=PE)
            EW("dv", PE, vtp)
            lgst["v_tp"] = CPY("dv", tok_sc[:, no * 32:(no + 1) * 32],
                               PSA[3][:, 0:32], ctr=DV)
            bank_set(3, DV, lgst["v_tp"])

        stages = [ln_stageA, ln_stageB, ln_stageC, ln_stageD]
        for no in range(4):
            emit_outproj_chunk(no, stages if no >= 1 else None, no - 1)
            if no >= 2:
                emit_logits(no - 2)
        for st_fn in stages:
            st_fn(3)
        emit_logits(2)
        emit_logits(3)
        v_spill_all = spill_vs[3]
        v_xn2_dv = xn2_dones[3]

        if stop in ("p4", "p5"):
            EW("pe", DV, v_xn2_dv)
            return finish()
        xn2 = xnv
        v_tok = lgst["v_tp"]
        def step_bar(v):
            EW("dv", DV, v)
        EW("ac", DV, v_tok)
        vexp = ACT(sc_scr[:, :], tok_sc[:, :], AF.Exp, ctr=AC)
        EW("dv", AC, vexp)
        v = 0
        for tt in range(16):
            v = EI("dv", lambda e, o=st8[:, tt:tt + 1],
                   i=sc_scr[:, tt * 8:(tt + 1) * 8]: e.reduce_sum(o, i, axis=AX.X), DV)
        step_bar(v)
        v = EI("dv", lambda e, o=st8[:, 16:32], i=st8[:, 0:16]: e.reciprocal(o, i), DV)
        step_bar(v)
        for tt in range(16):
            v = EI("dv", lambda e, o=sc_scr[:, tt * 8:(tt + 1) * 8],
                   i=sc_scr[:, tt * 8:(tt + 1) * 8], s=st8[:, 16 + tt:17 + tt]:
                   e.tensor_scalar_mul(o, i, s), DV)
        step_bar(v)
        for tt in range(16):
            v = EI("dv", lambda e, o=st8[:, tt:tt + 1],
                   i=sc_scr[:, tt * 8:(tt + 1) * 8]: e.reduce_max(o, i, axis=AX.X), DV)
        step_bar(v)
        for tt in range(16):
            v = EI("dv", lambda e, o=w_sc[:, tt * 8:(tt + 1) * 8],
                   i=sc_scr[:, tt * 8:(tt + 1) * 8], s=st8[:, tt:tt + 1]:
                   e.tensor_single_scalar(o, i, s, OP.is_ge), DV)
        step_bar(v)
        v = STT("dv", w_sc[:, :], w_sc[:, :], -1e30, sc_scr[:, :], OP.mult, OP.add,
                ctr=DV)
        step_bar(v)
        for tt in range(16):
            v = EI("dv", lambda e, o=st8[:, tt:tt + 1],
                   i=w_sc[:, tt * 8:(tt + 1) * 8]: e.reduce_max(o, i, axis=AX.X), DV)
        step_bar(v)
        for tt in range(16):
            v = EI("dv", lambda e, o=w_sc[:, tt * 8:(tt + 1) * 8],
                   i=sc_scr[:, tt * 8:(tt + 1) * 8], s=st8[:, tt:tt + 1]:
                   e.tensor_single_scalar(o, i, s, OP.is_ge), DV)
        step_bar(v)
        vsc_dv = TT("dv", w_sc[:, :], w_sc[:, :], sc_scr[:, :], OP.mult, ctr=DV)
        wT_ev = 0
        EW("pe", DV, vsc_dv)
        bank_wait(3)
        for no in range(4):
            if wT_ev:
                EW("pe", DV, wT_ev)
            vtp = 0
            for i in range(4):
                tt = no * 4 + i
                vtp = EI("pe", lambda e, o=PSA[3][0:8, i * 128:(i + 1) * 128],
                         s=w_sc[:, tt * 8:(tt + 1) * 8], idn=ident_f[:, :]:
                         e.transpose(o, s, idn), ctr=PE)
            EW("dv", PE, vtp)
            wT_ev = CPY("dv", w_eT[:, no * 512:(no + 1) * 512], PSA[3][0:8, 0:512],
                        ctr=DV)
        bank_set(3, DV, wT_ev)
        sw_ev = 0
        bank_wait(6)
        for (c0, cn) in CH_T:
            if sw_ev:
                EW("pe", AC, sw_ev)
            for kt in range(KT):
                MM(PSA[6][0:1, 0:cn], seg_bf[:, kt:kt + 1], xn2(kt, c0, cn),
                   kt == 0, kt == KT - 1, ctr=PE if kt == KT - 1 else None)
            EW("ac", PE, PE.n)
            sw_ev = ACT(r_pmk(c0, cn), PSA[6][0:1, 0:cn], AF.Sigmoid, ctr=AC)
        bank_set(6, AC, sw_ev)

        if stop == "p5b":
            return finish()
        v_attn_pe = PE.n
        GUSLOT = [50176, 52224]
        gu_ld = {}
        gu_rel = {}
        def emit_gu_load(e, j):
            idx = e * PJ + j
            if idx >= 2:
                EW("gp", PE, gu_rel[idx - 2])
            else:
                EW("gp", PE, v_attn_pe)
            slot = GUSLOT[idx % 2]
            gu_ld[idx] = DMA("gp", GUS[idx % 2],
                             abf(slot, 2048).rearrange("p (k m) -> p k m", k=KT),
                             dr["gu_all"][e, :, j * 256:(j + 1) * 256]
                             .rearrange("(k p) m -> p k m", p=128))

        db_ld = [0] * NE
        db_rel = [0] * NE
        def emit_db_tile(e, kt):
            if kt == 0:
                if e == 0:
                    EW("gp", PE, v_attn_pe)
                else:
                    EW("gp", PE, db_rel[e - 1])
            db_ld[e] = DMA("gp", GDN, abf(F_DB + kt * 1024, 1024),
                           dr["down_all"][e, kt * 128:(kt + 1) * 128, :])

        emit_gu_load(0, 0); emit_gu_load(0, 1)
        moe_add_dv = 0
        hs_war_dv = [0, 0, 0, 0]
        eo_war = [0, 0]
        wb_evs = [0] * NE
        wrow_sps = [0] * NE

        def emit_wrow(e):
            if e < 8:
                EW("sp", DV, wT_ev)
                if e > 0:
                    EW("sp", AC, wb_evs[e - 1])
                wrow_sps[e] = DMA("sp", WRO, r_g(0, T), w_eT[e:e + 1, :])

        def emit_wb(e, war_dv=0):
            wb_ev_chunks = 0
            if war_dv:
                EW("pe", DV, war_dv)
            for ci, (c0, cn) in enumerate(CH_T):
                if e < 8:
                    EW("pe", WRO, wrow_sps[e])
                else:
                    EW("pe", AC, sw_ev)
                wsrc = r_g(c0, cn) if e < 8 else r_pmk(c0, cn)
                bank_wait(4)
                vb = MM(PSA[4][:, 0:cn], ones_r(0), wsrc, True, True, ctr=PE)
                EW("ac", PE, vb)
                wb_ev_chunks = ACT(abf(F_WB + c0, cn), PSA[4][:, 0:cn], AF.Copy, ctr=AC)
                bank_set(4, AC, wb_ev_chunks)
            wb_evs[e] = wb_ev_chunks

        emit_wrow(0)
        emit_wb(0)

        p7_state = {"ost": [0, 0], "out_prev": 0}

        def emit_p7_chunk(no, ready_dv):
            c0, cn = CH_T[no]
            EW("pe", DV, ready_dv)
            EW("ac", DV, ready_dv)
            if no > 0:
                EW("gp", DV, p7_state["out_prev"])
            else:
                EW("gp", DV, ready_dv)
                EW("gp", SPL, spill_vs[3])
            xm_ld = DMA("gp", GX, abf(F_XM, 4096).rearrange("p (k t) -> p k t", k=KT),
                        xmid_d[:, c0:c0 + cn].rearrange("(k p) t -> p k t", p=128))
            vms = ln_stats_chunk(lambda mo, a, b: abf(F_MOE + mo * 2048 + a, b),
                                 c0, cn, X2B)
            ln_finish_chunk(cn, vms)
            EW("dv", GX, xm_ld)
            out_dv_prev = 0
            for mo in range(MO):
                osl = (no * MO + mo) % 2
                if p7_state["ost"][osl]:
                    EW("dv", OS[osl], p7_state["ost"][osl])
                moff = F_MOE + mo * 2048 + c0
                TT("dv", abf(H_T2, cn), abf(moff, cn), abf(H_RB, cn), OP.mult)
                TT("dv", abf(H_T2, cn), abf(H_T2, cn), abf(H_MRB, cn), OP.subtract)
                vout = TT("dv", afp(osl * 512, cn), abf(F_XM + mo * 512, cn),
                          abf(H_T2, cn), OP.add, ctr=DV)
                out_dv_prev = vout
                EW("sp", DV, vout)
                p7_state["ost"][osl] = DMA("sp", OS[osl],
                                           out_d[mo * 128:(mo + 1) * 128, c0:c0 + cn],
                                           afp(osl * 512, cn))
            p7_state["out_prev"] = out_dv_prev
            ln_war[0] = out_dv_prev

        for e in range(NE):
            wb_ev = wb_evs[e]
            h_done_dv = 0
            h_p9 = 0
            h10_ci = [0] * 4
            for j in range(PJ):
                idx = e * PJ + j
                slot = GUSLOT[idx % 2]
                EW("pe", GUS[idx % 2], gu_ld[idx])
                for ci in range(4):
                    bank_wait(ci)
                vg = [0] * 4
                for kt in range(KT):
                    for ci, (c0, cn) in enumerate(CH_T):
                        v = MM(PSA[ci][:, 0:cn], abf(slot + kt * 256, 128),
                               xn2(kt, c0, cn), kt == 0, kt == KT - 1,
                               ctr=PE if kt == KT - 1 else None)
                        if kt == KT - 1:
                            vg[ci] = v
                for ci in range(4):
                    bank_wait(4 + ci)
                vu = [0] * 4
                for kt in range(KT):
                    for ci, (c0, cn) in enumerate(CH_T):
                        v = MM(PSA[4 + ci][:, 0:cn], abf(slot + kt * 256 + 128, 128),
                               xn2(kt, c0, cn), kt == 0, kt == KT - 1,
                               ctr=PE if kt == KT - 1 else None)
                        if kt == KT - 1:
                            vu[ci] = v
                gu_rel[idx] = vu[-1]
                emit_db_tile(e, j)
                if idx + 2 < NE * PJ:
                    e2, j2 = divmod(idx + 2, PJ)
                    emit_gu_load(e2, j2)
                vus = [0] * 4
                for ci, (c0, cn) in enumerate(CH_T):
                    hs = F_XM + ci * 512
                    EW("ac", PE, vg[ci])
                    if hs_war_dv[ci]:
                        EW("ac", DV, hs_war_dv[ci])
                    vhs = ACT(abf(hs, cn), PSA[ci][:, 0:cn], AF.Silu, ctr=AC)
                    bank_set(ci, AC, vhs)
                for ci, (c0, cn) in enumerate(CH_T):
                    us = F_XM + 2048 + ci * 512
                    EW("ac", PE, vu[ci])
                    vus[ci] = ACT(abf(us, cn), PSA[4 + ci][:, 0:cn], AF.Copy, ctr=AC)
                    bank_set(4 + ci, AC, vus[ci])
                for ci, (c0, cn) in enumerate(CH_T):
                    hs = F_XM + ci * 512
                    us = F_XM + 2048 + ci * 512
                    EW("dv", AC, vus[ci])
                    if j == 0 and ci == 0:
                        EW("dv", AC, wb_ev)
                    hoff = F_H + j * 2048 + c0
                    hs_war_dv[ci] = TT("dv", abf(hoff, cn), abf(hs, cn),
                                       abf(us, cn), OP.mult, ctr=DV)
                    h_done_dv = TT("dv", abf(hoff, cn), abf(hoff, cn),
                                   abf(F_WB + c0, cn), OP.mult, ctr=DV)
                    if j == PJ - 1:
                        h10_ci[ci] = h_done_dv
                if j == PJ - 2:
                    h_p9 = h_done_dv
            EW("pe", GDN, db_ld[e])
            EW("pe", DV, h_p9)
            if e + 1 < NE:
                emit_wrow(e + 1)
            if e < NE - 1 or (stop is not None and str(stop).startswith("p6")):
                for mo in range(MO):
                    for ci, (c0, cn) in enumerate(CH_T):
                        bk = (mo * 4 + ci) % 4
                        bank_wait(bk)
                        vd = 0
                        for kt in range(KI):
                            if mo == 0 and kt == KI - 1:
                                EW("pe", DV, h10_ci[ci])
                            vd = MM(PSA[bk][:, 0:cn],
                                    abf(F_DB + kt * 1024 + mo * 128, 128),
                                    abf(F_H + kt * 2048 + c0, cn), kt == 0,
                                    kt == KI - 1, ctr=PE if kt == KI - 1 else None)
                        if mo == MO - 1 and ci == 3:
                            db_rel[e] = vd
                        moff = F_MOE + mo * 2048 + c0
                        eo = H_T2 if bk % 2 == 0 else H_RB
                        if e == 0:
                            EW("dv", PE, vd)
                            vev = CPY("dv", abf(moff, cn), PSA[bk][:, 0:cn], ctr=DV)
                            bank_set(bk, DV, vev)
                            moe_add_dv = vev
                        else:
                            EW("ac", PE, vd)
                            if eo_war[bk % 2]:
                                EW("ac", DV, eo_war[bk % 2])
                            vev = ACT(abf(eo, cn), PSA[bk][:, 0:cn], AF.Copy,
                                      ctr=AC)
                            bank_set(bk, AC, vev)
                            EW("dv", AC, vev)
                            moe_add_dv = TT("dv", abf(moff, cn), abf(moff, cn),
                                            abf(eo, cn), OP.add, ctr=DV)
                            eo_war[bk % 2] = moe_add_dv
                    if mo == 1 and e + 1 < NE:
                        emit_wb(e + 1, war_dv=h10_ci[3])
            else:
                madds = [0] * 4
                for ci, (c0, cn) in enumerate(CH_T):
                    madd = 0
                    for mo in range(MO):
                        bk = mo % 4
                        bank_wait(bk)
                        vd = 0
                        for kt in range(KI):
                            if mo == 0 and kt == KI - 1:
                                EW("pe", DV, h10_ci[ci])
                            vd = MM(PSA[bk][:, 0:cn],
                                    abf(F_DB + kt * 1024 + mo * 128, 128),
                                    abf(F_H + kt * 2048 + c0, cn), kt == 0,
                                    kt == KI - 1, ctr=PE if kt == KI - 1 else None)
                        if ci == 3 and mo == MO - 1:
                            db_rel[e] = vd
                        moff = F_MOE + mo * 2048 + c0
                        EW("dv", PE, vd)
                        madd = TT("dv", abf(moff, cn), abf(moff, cn),
                                  PSA[bk][:, 0:cn], OP.add, ctr=DV)
                        bank_set(bk, DV, madd)
                    madds[ci] = madd
                    if ci >= 1:
                        emit_p7_chunk(ci - 1, madds[ci - 1])
                emit_p7_chunk(3, madds[3])
            if stop == "p6e%d" % e:
                return finish()

        if stop == "p6":
            return finish()
        EW("sp", OS[0], OS[0].n)
        EW("sp", OS[1], OS[1].n)
        return finish()


_CACHE = {}


def _prep(inputs):
    f32 = np.float32
    x = np.asarray(inputs["L0_attn_x"], f32)
    xT = np.ascontiguousarray(x.transpose(0, 2, 1))

    shared = {
        "w_in_T": np.ascontiguousarray(np.asarray(inputs["L0_attn_in_proj_w"], f32).T),
        "conv_w": np.ascontiguousarray(
            np.asarray(inputs["L0_attn_conv_w"], f32)[:, 0, :]
            .reshape(16, 128, 3).transpose(1, 0, 2).reshape(128, 48)),
        "conv_b": np.ascontiguousarray(
            np.asarray(inputs["L0_attn_conv_b"], f32).reshape(16, 128).T),
        "d_param": np.ascontiguousarray(
            np.asarray(inputs["L0_attn_D_param"], f32).reshape(16, 128).T),
        "dt_b": np.ascontiguousarray(
            np.asarray(inputs["L0_attn_dt_proj_b"], f32).reshape(16, 128).T),
        "xpw": np.ascontiguousarray(np.asarray(inputs["L0_attn_x_proj_w"], f32)[0:64]),
        "dtw": np.ascontiguousarray(np.asarray(inputs["L0_attn_dt_proj_w"], f32)),
        "w_out_T": np.ascontiguousarray(np.asarray(inputs["L0_attn_out_proj_w"], f32).T),
        "gw_T": np.ascontiguousarray(np.asarray(inputs["L0_ffn_gate_w"], f32).T),
        "seg_T": np.ascontiguousarray(
            np.asarray(inputs["L0_ffn_shared_expert_gate_w"], f32).T),
        "ident": np.eye(128, dtype=f32),
        "ones_row": np.ones((65, 128), f32),
        "ones_col": np.ones((128, 1), f32),
    }
    egu = np.asarray(inputs["L0_ffn_experts_gate_up"], f32)
    edn = np.asarray(inputs["L0_ffn_experts_down"], f32)
    sg = np.asarray(inputs["L0_ffn_shared_gate_w"], f32)
    su = np.asarray(inputs["L0_ffn_shared_up_w"], f32)
    sd = np.asarray(inputs["L0_ffn_shared_down_w"], f32)
    gu_all = np.empty((NE, D, GU), f32)
    gu_all[0:8] = egu.transpose(0, 2, 1)
    gu_all[8] = np.concatenate([sg[0:IH], su[0:IH]], axis=0).T
    gu_all[9] = np.concatenate([sg[IH:], su[IH:]], axis=0).T
    perm = np.empty(GU, np.int64)
    for j in range(PJ):
        perm[j * 256: j * 256 + 128] = np.arange(j * 128, (j + 1) * 128)
        perm[j * 256 + 128: (j + 1) * 256] = IH + np.arange(j * 128, (j + 1) * 128)
    gu_all = gu_all[:, :, perm]
    down_all = np.empty((NE, IH, D), f32)
    down_all[0:8] = edn.transpose(0, 2, 1)
    down_all[8] = sd[:, 0:IH].T
    down_all[9] = sd[:, IH:].T
    shared["gu_all"] = np.ascontiguousarray(gu_all)
    shared["down_all"] = np.ascontiguousarray(down_all)

    in_maps = []
    for core in range(8):
        b, q = divmod(core, 4)
        t0 = q * T
        lo, hi = t0 - (W + 1), t0 + T + 1
        xwin = np.zeros((D, XW), f32)
        s, e2 = max(lo, 0), min(hi, SEQ)
        xwin[:, s - lo: s - lo + (e2 - s)] = xT[b][:, s:e2]
        pmask = np.ones((1, TW), f32)
        if q == 0:
            pmask[:, 0:W] = 0.0
        m = dict(shared)
        m["xw"] = xwin
        m["pmask"] = pmask
        in_maps.append(m)
    return in_maps


def kernel(**inputs):
    in_maps = _prep(inputs)
    if "nc" not in _CACHE:
        _CACHE["nc"] = _build()
    res = run_bass_kernel_spmd(_CACHE["nc"], in_maps, core_ids=list(range(8)))

    out = np.empty((B, SEQ, D), np.float32)
    for core in range(8):
        b, q = divmod(core, 4)
        out[b, q * T:(q + 1) * T, :] = res.results[core]["out"].T
    return out
